# version 01
# baseline (speedup 1.0000x reference)
import numpy as np
import ml_dtypes
import concourse.bass as bass
import concourse.mybir as mybir
from concourse.bass_utils import run_bass_kernel_spmd
from contextlib import ExitStack

F32 = mybir.dt.float32
BF16 = mybir.dt.bfloat16
I32 = mybir.dt.int32
AF = mybir.ActivationFunctionType
ALU = mybir.AluOpType
BF = ml_dtypes.bfloat16

S, E, H, KVH, D = 8192, 4096, 32, 8, 128
NB, BLK = 8, 1024
INTER = 11008
NCORES = 8
SCALE = float(1.0 / np.sqrt(128.0))
EXP_SHIFT = 12.0
EPS = 1e-5
WINDOW = 1024

ECH = E // 128
NPAN = (INTER + 511) // 512
RTD = 11
RTD_TAIL = INTER // 8 - 10 * 128
NVB = 56

_PHASES = ("W", "A", "B1", "B2", "B3", "B4", "C1", "C2")


def _mask_tiles():
    mask_list = []
    tiles = [[] for _ in range(NCORES)]
    k_idx = np.arange(128)[:, None]
    q_idx = np.arange(512)[None, :]
    for Bb in range(8, 64):
        for half in (0, 1):
            core_tiles = []
            any_zero = False
            for c in range(NCORES):
                j_g = (128 * Bb + k_idx + BLK * c) % S
                i_g = BLK * c + 512 * half + q_idx
                keep = (np.abs(i_g - j_g) >= WINDOW).astype(np.float32)
                core_tiles.append(keep)
                any_zero |= bool((keep == 0).any())
            if any_zero:
                mask_list.append((Bb, half))
                for c in range(NCORES):
                    tiles[c].append(core_tiles[c])
    return mask_list, [np.stack(t, axis=0) for t in tiles]


_CACHE = {}


def _masks():
    if "m" not in _CACHE:
        _CACHE["m"] = _mask_tiles()
    return _CACHE["m"]


def _consts():
    if "c" in _CACHE:
        return _CACHE["c"]
    inv_freq = 1.0 / (10000.0 ** (np.arange(0, D, 2, dtype=np.float64) / D))
    t = np.arange(S, dtype=np.float64)
    freqs = t[:, None] * inv_freq[None, :]
    emb = np.concatenate([freqs, freqs], axis=1)
    cosT = np.cos(emb).astype(np.float32).T.copy()
    sinT = np.sin(emb).astype(np.float32).T.copy()
    ident = np.eye(128, dtype=np.float32)
    RT = np.zeros((128, 128), dtype=np.float32)
    for d in range(64):
        RT[d + 64, d] = -1.0
        RT[d, d + 64] = 1.0
    indt_gu = np.zeros((32, 4 * 128), dtype=np.float32)
    for R in range(4):
        for p in range(128):
            indt_gu[8 * R + p // 16, 128 * R + p] = 1.0
    indt_d = np.zeros((86, RTD * 128), dtype=np.float32)
    for rt in range(RTD):
        cnt = 128 if rt < 10 else RTD_TAIL
        for p in range(cnt):
            indt_d[8 * rt + p // 16, 128 * rt + p] = 1.0
    _CACHE["c"] = (cosT, sinT, ident, RT, indt_gu, indt_d)
    return _CACHE["c"]


class Sems:
    def __init__(self, nc, ctx, names):
        self.h, self.c = {}, {}
        for n in names:
            self.h[n] = ctx.enter_context(nc.semaphore(n))
            self.c[n] = 0

    def __getitem__(self, n):
        return self.h[n]

    def inc(self, n, v=1):
        self.c[n] += v
        return self.c[n]

    def val(self, n):
        return self.c[n]


def build_program(upto="C2", dbg=False):
    nphase = _PHASES.index(upto) + 1
    phases = set(_PHASES[:nphase])
    nc = bass.Bass("TRN2", target_bir_lowering=False, debug=False)
    MASK_LIST, _ = _masks()
    NMASK = len(MASK_LIST)
    MASK_IDX = {bh: i for i, bh in enumerate(MASK_LIST)}

    scratch_kind = "ExternalOutput" if dbg else "Internal"

    x_in = nc.dram_tensor("x_rot", [S, E], F32, kind="ExternalInput").ap()
    wq = nc.dram_tensor("wq", [H * D, E], F32, kind="ExternalInput").ap()
    wk = nc.dram_tensor("wk", [KVH * D, E], F32, kind="ExternalInput").ap()
    wv = nc.dram_tensor("wv", [KVH * D, E], F32, kind="ExternalInput").ap()
    wo = nc.dram_tensor("wo", [E, H * D], F32, kind="ExternalInput").ap()
    qw_g = nc.dram_tensor("qw_g", [E // 8, INTER], I32, kind="ExternalInput").ap()
    sc_g = nc.dram_tensor("sc_g", [32, INTER], F32, kind="ExternalInput").ap()
    qz_g = nc.dram_tensor("qz_g", [32, INTER // 8], I32, kind="ExternalInput").ap()
    qw_u = nc.dram_tensor("qw_u", [E // 8, INTER], I32, kind="ExternalInput").ap()
    sc_u = nc.dram_tensor("sc_u", [32, INTER], F32, kind="ExternalInput").ap()
    qz_u = nc.dram_tensor("qz_u", [32, INTER // 8], I32, kind="ExternalInput").ap()
    qw_d = nc.dram_tensor("qw_d", [INTER // 8, E], I32, kind="ExternalInput").ap()
    sc_d = nc.dram_tensor("sc_d", [86, E], F32, kind="ExternalInput").ap()
    qz_d = nc.dram_tensor("qz_d", [86, E // 8], I32, kind="ExternalInput").ap()
    cos_in = nc.dram_tensor("cosT", [128, S], F32, kind="ExternalInput").ap()
    sin_in = nc.dram_tensor("sinT", [128, S], F32, kind="ExternalInput").ap()
    ident_in = nc.dram_tensor("ident", [128, 128], F32, kind="ExternalInput").ap()
    rot_in = nc.dram_tensor("rotm", [128, 128], F32, kind="ExternalInput").ap()
    indtgu_in = nc.dram_tensor("indt_gu", [32, 4 * 128], F32,
                               kind="ExternalInput").ap()
    indtd_in = nc.dram_tensor("indt_d", [86, RTD * 128], F32,
                              kind="ExternalInput").ap()
    masks_in = nc.dram_tensor("masks", [128, max(NMASK, 1) * 512], F32,
                              kind="ExternalInput").ap()
    out = nc.dram_tensor("out", [BLK, E], F32, kind="ExternalOutput").ap()

    wqt_d = nc.dram_tensor("wqt_d", [H * D // 128, 128, ECH, 128], BF16,
                           kind=scratch_kind).ap()
    wkt_d = nc.dram_tensor("wkt_d", [KVH * D // 128, 128, ECH, 128], BF16,
                           kind=scratch_kind).ap()
    wvt_d = nc.dram_tensor("wvt_d", [KVH * D // 128, 128, ECH, 128], BF16,
                           kind=scratch_kind).ap()
    wot_d = nc.dram_tensor("wot_d", [E // 128, 128, H * D // 128, 128], BF16,
                           kind=scratch_kind).ap()
    kt_d = nc.dram_tensor("kt_d", [KVH * D, S], BF16, kind=scratch_kind).ap()
    v_d = nc.dram_tensor("v_d", [KVH, S // 128, 128, 128], BF16,
                         kind=scratch_kind).ap()
    xt_d = nc.dram_tensor("xt_d", [E, BLK], F32, kind=scratch_kind).ap()
    qt_dbg = (nc.dram_tensor("qt_dbg", [H * D, BLK], BF16, kind="ExternalOutput").ap()
              if dbg else None)
    xnt_dbg = (nc.dram_tensor("xnt_dbg", [E, BLK], BF16, kind="ExternalOutput").ap()
               if dbg else None)
    aot_dbg = (nc.dram_tensor("aot_dbg", [H * D, BLK], BF16,
                              kind="ExternalOutput").ap() if dbg else None)
    xmid_d = nc.dram_tensor("xmid_d", [E, BLK], F32, kind=scratch_kind).ap()
    xn2_d = nc.dram_tensor("xn2_d", [E, BLK], BF16, kind=scratch_kind).ap()
    ht_d = nc.dram_tensor("ht_d", [INTER, BLK], BF16, kind=scratch_kind).ap()

    ctx = ExitStack()
    with ctx:
        sems = Sems(nc, ctx, ["ld", "ldg", "sv", "dve", "act", "pe", "pool"])

        def dma(eng, out_, in_, sem="ld", **kw):
            eng.dma_start(out=out_, in_=in_, **kw).then_inc(sems[sem], 16)
            return sems.inc(sem, 16)

        def guard(eng, *names):
            for n in names:
                if sems.val(n) > 0:
                    eng.wait_ge(sems[n], sems.val(n))

        sb = lambda n, s_, d_: ctx.enter_context(nc.sbuf_tensor(n, s_, d_))

        identb = sb("identb", [128, 128], BF16)
        identf = sb("identf", [128, 128], F32)
        rotf = sb("rotf", [128, 128], F32)
        ones_col_b = sb("ones_col_b", [128, 1], BF16)
        ones_row_f = sb("ones_row_f", [1, 128], F32)
        ones_col_f = sb("ones_col_f", [128, 1], F32)
        ebias = sb("ebias", [128, 1], F32)
        big1 = sb("big1", [128, 32768], BF16)
        big2 = sb("big2", [128, 32768], BF16)

        dma(nc.sync, identf[:], ident_in)
        dma(nc.sync, rotf[:], rot_in)
        dma(nc.gpsimd, identb[:], ident_in, "ldg")
        nc.gpsimd.wait_ge(sems["ldg"], sems.val("ldg"))
        nc.gpsimd.memset(ones_col_b[:], 1.0)
        nc.gpsimd.memset(ones_row_f[:], 1.0)
        nc.gpsimd.memset(ones_col_f[:], 1.0)
        nc.gpsimd.memset(ebias[:], -EXP_SHIFT)
        nc.sync.wait_ge(sems["ld"], sems.val("ld"))
        nc.all_engine_barrier()

        if "W" in phases:
            with (
                nc.sbuf_tensor("w_nat", [128, E], BF16) as w_nat,
                nc.sbuf_tensor("w_stage", [128, 512], BF16) as w_stage,
                nc.psum_tensor("w_ps", [128, 512], BF16) as w_ps,
            ):
                def transpose_weight(w_src, wt_dst, n_tiles, n_free):
                    nfc = n_free // 128
                    for t in range(n_tiles):
                        guard(nc.gpsimd, "pe")
                        lv = dma(nc.gpsimd, w_nat[:, :n_free],
                                 w_src[128 * t:128 * t + 128, :], "ldg")
                        nc.tensor.wait_ge(sems["ldg"], lv)
                        for cg in range(nfc // 4):
                            guard(nc.tensor, "dve")
                            for i in range(4):
                                c = 4 * cg + i
                                _li = nc.tensor.transpose(
                                    w_ps[:, 128 * i:128 * i + 128],
                                    w_nat[:, 128 * c:128 * c + 128], identb[:])
                            _li.then_inc(sems["pe"], 1)
                            pv = sems.inc("pe")
                            nc.vector.wait_ge(sems["pe"], pv)
                            guard(nc.vector, "sv")
                            nc.vector.tensor_copy(
                                out=w_stage[:], in_=w_ps[:]).then_inc(sems["dve"], 1)
                            dv = sems.inc("dve")
                            nc.sync.wait_ge(sems["dve"], dv)
                            dst = wt_dst[t, :, 4 * cg:4 * cg + 4, :]
                            src = w_stage[:].rearrange("p (cc n) -> p cc n", cc=4)
                            dma(nc.sync, dst, src, "sv")

                transpose_weight(wq, wqt_d, H * D // 128, E)
                transpose_weight(wk, wkt_d, KVH * D // 128, E)
                transpose_weight(wv, wvt_d, KVH * D // 128, E)
                transpose_weight(wo, wot_d, E // 128, H * D)
                nc.sync.wait_ge(sems["sv"], sems.val("sv"))
            nc.all_engine_barrier()

        with ExitStack() as es:
            x_sub = es.enter_context(nc.sbuf_tensor("x_sub", [128, E], F32))
            xn_sub = es.enter_context(nc.sbuf_tensor("xn_sub", [128, E], BF16))
            st6 = es.enter_context(nc.sbuf_tensor("st6", [128, 48], F32))
            st2 = es.enter_context(nc.sbuf_tensor("st2", [128, 2], F32))
            negmu = es.enter_context(nc.sbuf_tensor("negmu", [128, 1], F32))
            spacer = es.enter_context(nc.sbuf_tensor("spacer", [128, 1], F32))
            rstd = es.enter_context(nc.sbuf_tensor("rstd", [128, 1], F32))
            wpan = es.enter_context(nc.sbuf_tensor("wpan", [128, ECH * 128], BF16))
            wpan2 = es.enter_context(nc.sbuf_tensor("wpan2", [128, ECH * 128], BF16))
            cosb = es.enter_context(nc.sbuf_tensor("cosb", [128, BLK], F32))
            sinb = es.enter_context(nc.sbuf_tensor("sinb", [128, BLK], F32))
            kf = es.enter_context(nc.sbuf_tensor("kf", [128, 512], F32))
            t1f = es.enter_context(nc.sbuf_tensor("t1f", [128, 512], F32))
            t2f = es.enter_context(nc.sbuf_tensor("t2f", [128, 512], F32))
            kvb = es.enter_context(nc.sbuf_tensor("kvb", [128, 512], BF16))
            vb_c = es.enter_context(nc.sbuf_tensor("vb_c", [128, 512], BF16))
            vstage = es.enter_context(nc.sbuf_tensor("vstage", [128, 512], BF16))
            xt_stage = es.enter_context(nc.sbuf_tensor("xt_stage", [128, 512], F32))
            ps_t = es.enter_context(nc.psum_tensor("ps_t", [128, 512], BF16))
            ps_tf = es.enter_context(nc.psum_tensor("ps_tf", [128, 512], F32))
            ps_k = es.enter_context(nc.psum_tensor("ps_k", [128, 512], F32))
            ps_v = es.enter_context(nc.psum_tensor("ps_v", [128, 512], F32))
            ps_r = es.enter_context(nc.psum_tensor("ps_r", [128, 512], F32))
            xnt_v = big1[:].rearrange("p (c m) -> p c m", c=ECH)
            qt_v = big2[:].rearrange("p (c m) -> p c m", c=ECH)
            wpan_v = wpan[:].rearrange("p (c n) -> p c n", c=ECH)
            wpan2_v = wpan2[:].rearrange("p (c n) -> p c n", c=ECH)

            def ln_block(b, also_xt):
                for ms in range(8):
                    row0 = BLK * b + 128 * ms
                    guard(nc.gpsimd, "pe")
                    lv = dma(nc.gpsimd, x_sub[:], x_in[row0:row0 + 128, :], "ldg")
                    nc.vector.wait_ge(sems["ldg"], lv)
                    for i in range(8):
                        nc.vector.bn_stats(out=st6[:, 6 * i:6 * i + 6],
                                           in_=x_sub[:, 512 * i:512 * i + 512])
                    nc.vector.tensor_copy(out=spacer[:], in_=ebias[:])
                    nc.vector.tensor_copy(out=spacer[:], in_=ebias[:])
                    nc.vector.bn_aggr(out=st2[:], in_=st6[:])
                    nc.vector.tensor_copy(out=spacer[:], in_=ebias[:])
                    nc.vector.tensor_copy(out=spacer[:], in_=ebias[:])
                    nc.vector.tensor_scalar(out=negmu[:], in0=st2[:, 0:1],
                                            scalar1=-1.0, scalar2=None, op0=ALU.mult)
                    nc.vector.tensor_scalar(
                        out=rstd[:], in0=st2[:, 1:2], scalar1=EPS, scalar2=None,
                        op0=ALU.add).then_inc(sems["dve"], 1)
                    dv = sems.inc("dve")
                    nc.scalar.wait_ge(sems["dve"], dv)
                    nc.scalar.activation(out=rstd[:], in_=rstd[:], func=AF.Ln)
                    nc.scalar.activation(out=rstd[:], in_=rstd[:], func=AF.Exp,
                                         scale=-0.5).then_inc(sems["act"], 1)
                    av = sems.inc("act")
                    nc.vector.wait_ge(sems["act"], av)
                    guard(nc.vector, "pe")
                    nc.vector.tensor_scalar(
                        out=xn_sub[:], in0=x_sub[:], scalar1=negmu[:, 0:1],
                        scalar2=rstd[:, 0:1], op0=ALU.add,
                        op1=ALU.mult).then_inc(sems["dve"], 1)
                    dv = sems.inc("dve")
                    nc.tensor.wait_ge(sems["dve"], dv)
                    for cg in range(8):
                        guard(nc.tensor, "dve")
                        for i in range(4):
                            c = 4 * cg + i
                            _li = nc.tensor.transpose(
                                ps_t[:, 128 * i:128 * i + 128],
                                xn_sub[:, 128 * c:128 * c + 128], identb[:])
                        _li.then_inc(sems["pe"], 1)
                        pv = sems.inc("pe")
                        nc.vector.wait_ge(sems["pe"], pv)
                        for i in range(4):
                            c = 4 * cg + i
                            _lc = nc.vector.tensor_copy(
                                out=xnt_v[:, c, 128 * ms:128 * ms + 128],
                                in_=ps_t[:, 128 * i:128 * i + 128])
                        _lc.then_inc(sems["dve"], 1)
                        sems.inc("dve")
                    if also_xt:
                        for cg in range(8):
                            guard(nc.tensor, "dve")
                            for i in range(4):
                                c = 4 * cg + i
                                _li = nc.tensor.transpose(
                                    ps_tf[:, 128 * i:128 * i + 128],
                                    x_sub[:, 128 * c:128 * c + 128], identf[:])
                            _li.then_inc(sems["pe"], 1)
                            pv = sems.inc("pe")
                            nc.vector.wait_ge(sems["pe"], pv)
                            guard(nc.vector, "sv")
                            nc.vector.tensor_copy(
                                out=xt_stage[:], in_=ps_tf[:]).then_inc(sems["dve"], 1)
                            dv = sems.inc("dve")
                            nc.sync.wait_ge(sems["dve"], dv)
                            dst = xt_d.rearrange("(cc p) m -> p cc m", p=128)[
                                :, 4 * cg:4 * cg + 4, 128 * ms:128 * ms + 128]
                            dma(nc.sync, dst,
                                xt_stage[:].rearrange("p (cc m) -> p cc m", cc=4),
                                "sv")

            def load_wt_panel(buf, wt_src, t):
                bufv = buf.rearrange("p (cc n) -> p cc n", cc=ECH)
                return dma(nc.sync, bufv, wt_src[t], "ld")

            def rope_apply(ps_src, half, out_bf):
                cs = cosb[:, 512 * half:512 * half + 512]
                sn = sinb[:, 512 * half:512 * half + 512]
                nc.vector.tensor_copy(out=kf[:], in_=ps_src[:]).then_inc(
                    sems["dve"], 1)
                dv = sems.inc("dve")
                nc.tensor.wait_ge(sems["dve"], dv)
                nc.tensor.matmul(ps_r[:], rotf[:], kf[:], start=True,
                                 stop=True).then_inc(sems["pe"], 1)
                pv = sems.inc("pe")
                nc.vector.tensor_mul(out=t1f[:], in0=kf[:], in1=cs)
                nc.vector.wait_ge(sems["pe"], pv)
                nc.vector.tensor_mul(out=t2f[:], in0=ps_r[:], in1=sn)
                nc.vector.tensor_add(out=out_bf, in0=t1f[:], in1=t2f[:]).then_inc(
                    sems["dve"], 1)
                return sems.inc("dve")

            if "A" in phases:
                for b in range(1, 8):
                    ln_block(b, also_xt=False)
                    guard(nc.sync, "dve")
                    dma(nc.sync, cosb[:], cos_in[:, BLK * b:BLK * b + BLK], "ld")
                    lv_cs = dma(nc.sync, sinb[:],
                                sin_in[:, BLK * b:BLK * b + BLK], "ld")
                    for t in range(KVH):
                        guard(nc.sync, "pe")
                        load_wt_panel(wpan[:], wkt_d, t)
                        lv = load_wt_panel(wpan2[:], wvt_d, t)
                        nc.tensor.wait_ge(sems["ld"], max(lv, lv_cs))
                        guard(nc.tensor, "dve")
                        for half in (0, 1):
                            for c in range(ECH):
                                _li = nc.tensor.matmul(
                                    ps_k[:], wpan_v[:, c, :],
                                    xnt_v[:, c, 512 * half:512 * half + 512],
                                    start=(c == 0), stop=(c == ECH - 1))
                            _li.then_inc(sems["pe"], 1)
                            pv = sems.inc("pe")
                            nc.vector.wait_ge(sems["pe"], pv)
                            guard(nc.vector, "sv")
                            dvk = rope_apply(ps_k, half, kvb[:])
                            nc.sync.wait_ge(sems["dve"], dvk)
                            dma(nc.sync,
                                kt_d[128 * t:128 * t + 128,
                                     BLK * b + 512 * half:
                                     BLK * b + 512 * half + 512],
                                kvb[:], "sv")
                            for c in range(ECH):
                                _li = nc.tensor.matmul(
                                    ps_v[:], wpan2_v[:, c, :],
                                    xnt_v[:, c, 512 * half:512 * half + 512],
                                    start=(c == 0), stop=(c == ECH - 1))
                            _li.then_inc(sems["pe"], 1)
                            pv = sems.inc("pe")
                            nc.scalar.wait_ge(sems["pe"], pv)
                            nc.scalar.copy(out=vb_c[:], in_=ps_v[:]).then_inc(
                                sems["act"], 1)
                            av = sems.inc("act")
                            nc.tensor.wait_ge(sems["act"], av)
                            guard(nc.tensor, "dve")
                            for i in range(4):
                                _li = nc.tensor.transpose(
                                    ps_t[:, 128 * i:128 * i + 128],
                                    vb_c[:, 128 * i:128 * i + 128], identb[:])
                            _li.then_inc(sems["pe"], 1)
                            pv = sems.inc("pe")
                            nc.vector.wait_ge(sems["pe"], pv)
                            guard(nc.vector, "sv")
                            nc.vector.tensor_copy(
                                out=vstage[:], in_=ps_t[:]).then_inc(sems["dve"], 1)
                            dv = sems.inc("dve")
                            nc.sync.wait_ge(sems["dve"], dv)
                            bb0 = (BLK * b + 512 * half) // 128
                            dst = v_d.rearrange("kv bb p n -> kv p bb n")[
                                t, :, bb0:bb0 + 4, :]
                            dma(nc.sync, dst,
                                vstage[:].rearrange("p (ss n) -> p ss n", ss=4),
                                "sv")
                if dbg:
                    guard(nc.sync, "dve")
                    for c in range(ECH):
                        dma(nc.sync, xnt_dbg[128 * c:128 * c + 128, :],
                            xnt_v[:, c, :], "sv")
                nc.sync.wait_ge(sems["sv"], sems.val("sv"))
                nc.all_engine_barrier()

            if "B1" in phases:
                ln_block(0, also_xt=True)
                guard(nc.sync, "dve")
                dma(nc.sync, cosb[:], cos_in[:, 0:BLK], "ld")
                lv_cs = dma(nc.sync, sinb[:], sin_in[:, 0:BLK], "ld")
                nc.sync.wait_ge(sems["sv"], sems.val("sv"))
                nc.all_engine_barrier()

            if "B2" in phases:
                for t in range(H):
                    guard(nc.sync, "pe")
                    lv = load_wt_panel(wpan[:], wqt_d, t)
                    nc.tensor.wait_ge(sems["ld"], max(lv, lv_cs))
                    guard(nc.tensor, "dve")
                    for half in (0, 1):
                        for c in range(ECH):
                            _li = nc.tensor.matmul(
                                ps_k[:], wpan_v[:, c, :],
                                xnt_v[:, c, 512 * half:512 * half + 512],
                                start=(c == 0), stop=(c == ECH - 1))
                        _li.then_inc(sems["pe"], 1)
                        pv = sems.inc("pe")
                        nc.vector.wait_ge(sems["pe"], pv)
                        rope_apply(ps_k, half,
                                   qt_v[:, t, 512 * half:512 * half + 512])
                if dbg:
                    guard(nc.sync, "dve")
                    for t in range(H):
                        dma(nc.sync, qt_dbg[128 * t:128 * t + 128, :],
                            qt_v[:, t, :], "sv")
                    nc.sync.wait_ge(sems["sv"], sems.val("sv"))
                nc.all_engine_barrier()

        if "B3" in phases:
            with ExitStack() as es:
                mask_sb = es.enter_context(nc.sbuf_tensor("mask_sb", [128, max(NMASK, 1) * 512], BF16))
                kt_all = es.enter_context(nc.sbuf_tensor("kt_all", [128, NVB * 128], BF16))
                v_all = es.enter_context(nc.sbuf_tensor("v_all", [128, NVB * 128], BF16))
                pt00 = es.enter_context(nc.sbuf_tensor("pt00", [128, 512], BF16))
                pt01 = es.enter_context(nc.sbuf_tensor("pt01", [128, 512], BF16))
                pt10 = es.enter_context(nc.sbuf_tensor("pt10", [128, 512], BF16))
                pt11 = es.enter_context(nc.sbuf_tensor("pt11", [128, 512], BF16))
                lrow = es.enter_context(nc.sbuf_tensor("lrow", [1, 1024], F32))
                rrow = es.enter_context(nc.sbuf_tensor("rrow", [1, 1024], F32))
                rb0 = es.enter_context(nc.sbuf_tensor("rb0", [128, 512], F32))
                rb1 = es.enter_context(nc.sbuf_tensor("rb1", [128, 512], F32))
                ps_s00 = es.enter_context(nc.psum_tensor("ps_s00", [128, 512], F32))
                ps_s01 = es.enter_context(nc.psum_tensor("ps_s01", [128, 512], F32))
                ps_s10 = es.enter_context(nc.psum_tensor("ps_s10", [128, 512], F32))
                ps_s11 = es.enter_context(nc.psum_tensor("ps_s11", [128, 512], F32))
                ps_o0 = es.enter_context(nc.psum_tensor("ps_o0", [128, 512], F32))
                ps_o1 = es.enter_context(nc.psum_tensor("ps_o1", [128, 512], F32))
                ps_l0 = es.enter_context(nc.psum_tensor("ps_l0", [128, 512], F32))
                ps_l1 = es.enter_context(nc.psum_tensor("ps_l1", [128, 512], F32))
                aot_v = big1[:].rearrange("p (c m) -> p c m", c=ECH)
                qt_v = big2[:].rearrange("p (c m) -> p c m", c=ECH)
                kt_v = kt_all[:].rearrange("p (b n) -> p b n", b=NVB)
                v_v = v_all[:].rearrange("p (b n) -> p b n", b=NVB)
                ps_s = [[ps_s00, ps_s01], [ps_s10, ps_s11]]
                pt = [[pt00, pt01], [pt10, pt11]]
                ps_o = [ps_o0, ps_o1]
                ps_l = [ps_l0, ps_l1]
                if NMASK:
                    dma(nc.gpsimd, mask_sb[:], masks_in, "ldg")
                    nc.gpsimd.wait_ge(sems["ldg"], sems.val("ldg"))
                nc.all_engine_barrier()

                for kv in range(KVH):
                    guard(nc.sync, "pe")
                    dma(nc.sync, kt_all[:], kt_d[128 * kv:128 * kv + 128, BLK:S],
                        "ld")
                    for bb in range(NVB):
                        lv = dma(nc.sync, v_v[:, bb, :], v_d[kv, 8 + bb], "ld")
                    nc.tensor.wait_ge(sems["ld"], lv)
                    for hq in range(4):
                        h = 4 * kv + hq
                        act_mark = {}
                        dve_mark = {}
                        pe_av = {}
                        guard(nc.scalar, "pe")
                        for i, Bb in enumerate(range(8, 64)):
                            par = i % 2
                            first, last = (i == 0), (i == 55)
                            if i >= 2:
                                nc.tensor.wait_ge(sems["act"], act_mark[i - 2])
                            for half in (0, 1):
                                nc.tensor.matmul(
                                    ps_s[half][par][:], kt_v[:, i, :],
                                    qt_v[:, h, 512 * half:512 * half + 512],
                                    start=True, stop=True).then_inc(sems["pe"], 1)
                                sems.inc("pe")
                            pe_s = sems.val("pe")
                            nc.scalar.wait_ge(sems["pe"], pe_s)
                            if i >= 2:
                                nc.scalar.wait_ge(sems["pe"], pe_av[i - 2])
                            for half in (0, 1):
                                nc.scalar.activation(
                                    out=pt[half][par][:], in_=ps_s[half][par][:],
                                    func=AF.Exp, bias=ebias[:, 0:1],
                                    scale=SCALE).then_inc(sems["act"], 1)
                                sems.inc("act")
                            act_mark[i] = sems.val("act")
                            masked = False
                            for half in (0, 1):
                                mi = MASK_IDX.get((Bb, half))
                                if mi is not None:
                                    nc.vector.wait_ge(sems["act"], act_mark[i])
                                    nc.vector.tensor_mul(
                                        out=pt[half][par][:],
                                        in0=pt[half][par][:],
                                        in1=mask_sb[:, 512 * mi:512 * mi + 512],
                                    ).then_inc(sems["dve"], 1)
                                    sems.inc("dve")
                                    masked = True
                            dve_mark[i] = sems.val("dve")
                            if masked:
                                nc.tensor.wait_ge(sems["dve"], dve_mark[i])
                            else:
                                nc.tensor.wait_ge(sems["act"], act_mark[i])
                            for half in (0, 1):
                                nc.tensor.matmul(
                                    ps_l[half][0:1, :], ones_col_b[:],
                                    pt[half][par][:], start=first, stop=last)
                            for half in (0, 1):
                                _li = nc.tensor.matmul(
                                    ps_o[half][:], v_v[:, i, :], pt[half][par][:],
                                    start=first, stop=last)
                            _li.then_inc(sems["pe"], 1)
                            pe_av[i] = sems.inc("pe")
                        nc.vector.wait_ge(sems["pe"], pe_av[55])
                        nc.vector.tensor_copy(out=lrow[:, 0:512],
                                              in_=ps_l0[0:1, :])
                        nc.vector.tensor_copy(out=lrow[:, 512:1024],
                                              in_=ps_l1[0:1, :])
                        nc.vector.reciprocal(out=rrow[:], in_=lrow[:]).then_inc(
                            sems["dve"], 1)
                        dv = sems.inc("dve")
                        nc.tensor.wait_ge(sems["dve"], dv)
                        nc.tensor.matmul(ps_l0[:], ones_row_f[:], rrow[:, 0:512],
                                         start=True, stop=True)
                        nc.tensor.matmul(ps_l1[:], ones_row_f[:],
                                         rrow[:, 512:1024], start=True,
                                         stop=True).then_inc(sems["pe"], 1)
                        pv = sems.inc("pe")
                        nc.vector.wait_ge(sems["pe"], pv)
                        nc.vector.tensor_copy(out=rb0[:], in_=ps_l0[:])
                        nc.vector.tensor_copy(out=rb1[:], in_=ps_l1[:])
                        nc.vector.tensor_mul(out=aot_v[:, h, 0:512],
                                             in0=ps_o0[:], in1=rb0[:])
                        nc.vector.tensor_mul(out=aot_v[:, h, 512:1024],
                                             in0=ps_o1[:], in1=rb1[:]).then_inc(
                            sems["dve"], 1)
                        dve_oev = sems.inc("dve")
                        nc.tensor.wait_ge(sems["dve"], dve_oev)
                if dbg:
                    guard(nc.sync, "dve")
                    for t in range(H):
                        dma(nc.sync, aot_dbg[128 * t:128 * t + 128, :],
                            aot_v[:, t, :], "sv")
                    nc.sync.wait_ge(sems["sv"], sems.val("sv"))
            nc.all_engine_barrier()

        if "B4" in phases:
            with ExitStack() as es:
                wpanB = es.enter_context(nc.sbuf_tensor("wpanB", [128, ECH * 128], BF16))
                xt_tile = es.enter_context(nc.sbuf_tensor("xt_tile", [128, 512], F32))
                xm_tile = es.enter_context(nc.sbuf_tensor("xm_tile", [128, 512], F32))
                sqst = es.enter_context(nc.sbuf_tensor("sqst", [128, 512], F32))
                mrow = es.enter_context(nc.sbuf_tensor("mrow", [1, 512], F32))
                qrow = es.enter_context(nc.sbuf_tensor("qrow", [1, 512], F32))
                vrow = es.enter_context(nc.sbuf_tensor("vrow", [1, 512], F32))
                nmrow = es.enter_context(nc.sbuf_tensor("nmrow", [1, 512], F32))
                rsrow = es.enter_context(nc.sbuf_tensor("rsrow", [1, 512], F32))
                nb_b = es.enter_context(nc.sbuf_tensor("nb_b", [128, 512], F32))
                rs_b = es.enter_context(nc.sbuf_tensor("rs_b", [128, 512], F32))
                xn2st = es.enter_context(nc.sbuf_tensor("xn2st", [128, 512], BF16))
                aot_v = big1[:].rearrange("p (c m) -> p c m", c=ECH)
                wpanB_v = wpanB[:].rearrange("p (c n) -> p c n", c=ECH)
                for half in (0, 1):
                    with (
                        nc.psum_tensor(f"ps_oo{half}", [128, 512], F32) as ps_oo,
                        nc.psum_tensor(f"ps_sum{half}", [1, 512], F32) as ps_sum,
                        nc.psum_tensor(f"ps_sq{half}", [1, 512], F32) as ps_sq,
                    ):
                        for et in range(ECH):
                            guard(nc.sync, "pe", "act", "dve")
                            dma(nc.sync,
                                wpanB[:].rearrange("p (cc n) -> p cc n", cc=ECH),
                                wot_d[et], "ld")
                            lv = dma(nc.sync, xt_tile[:],
                                     xt_d[128 * et:128 * et + 128,
                                          512 * half:512 * half + 512], "ld")
                            nc.tensor.wait_ge(sems["ld"], lv)
                            guard(nc.tensor, "dve", "act")
                            for c in range(ECH):
                                _li = nc.tensor.matmul(
                                    ps_oo[:], wpanB_v[:, c, :],
                                    aot_v[:, c, 512 * half:512 * half + 512],
                                    start=(c == 0), stop=(c == ECH - 1))
                            _li.then_inc(sems["pe"], 1)
                            pv = sems.inc("pe")
                            nc.vector.wait_ge(sems["pe"], pv)
                            nc.vector.tensor_add(
                                out=xm_tile[:], in0=ps_oo[:],
                                in1=xt_tile[:]).then_inc(sems["dve"], 1)
                            dv = sems.inc("dve")
                            nc.sync.wait_ge(sems["dve"], dv)
                            dma(nc.sync,
                                xmid_d[128 * et:128 * et + 128,
                                       512 * half:512 * half + 512],
                                xm_tile[:], "sv")
                            nc.scalar.wait_ge(sems["dve"], dv)
                            nc.scalar.square(out=sqst[:], in_=xm_tile[:]).then_inc(
                                sems["act"], 1)
                            av = sems.inc("act")
                            nc.tensor.wait_ge(sems["act"], av)
                            nc.tensor.matmul(ps_sum[:], ones_col_f[:],
                                             xm_tile[:], start=(et == 0),
                                             stop=(et == ECH - 1))
                            nc.tensor.matmul(ps_sq[:], ones_col_f[:], sqst[:],
                                             start=(et == 0),
                                             stop=(et == ECH - 1)).then_inc(
                                sems["pe"], 1)
                            sems.inc("pe")
                        guard(nc.vector, "pe")
                        nc.vector.tensor_scalar(out=mrow[:], in0=ps_sum[0:1, :],
                                                scalar1=1.0 / E, scalar2=None,
                                                op0=ALU.mult)
                        nc.vector.tensor_scalar(out=qrow[:], in0=ps_sq[0:1, :],
                                                scalar1=1.0 / E, scalar2=None,
                                                op0=ALU.mult)
                        nc.vector.tensor_mul(out=vrow[:], in0=mrow[:], in1=mrow[:])
                        nc.vector.tensor_sub(out=vrow[:], in0=qrow[:], in1=vrow[:])
                        nc.vector.tensor_scalar(out=vrow[:], in0=vrow[:],
                                                scalar1=EPS, scalar2=None,
                                                op0=ALU.add)
                        nc.vector.tensor_scalar(out=nmrow[:], in0=mrow[:],
                                                scalar1=-1.0, scalar2=None,
                                                op0=ALU.mult).then_inc(sems["dve"], 1)
                        dv = sems.inc("dve")
                        nc.scalar.wait_ge(sems["dve"], dv)
                        nc.scalar.activation(out=vrow[:], in_=vrow[:], func=AF.Ln)
                        nc.scalar.activation(out=rsrow[:], in_=vrow[:], func=AF.Exp,
                                             scale=-0.5).then_inc(sems["act"], 1)
                        av = sems.inc("act")
                        nc.tensor.wait_ge(sems["act"], av)
                    with (
                        nc.psum_tensor(f"ps_b0{half}", [128, 512], F32) as ps_b0,
                        nc.psum_tensor(f"ps_b1{half}", [128, 512], F32) as ps_b1,
                    ):
                        nc.tensor.matmul(ps_b0[:], ones_row_f[:], nmrow[:],
                                         start=True, stop=True)
                        nc.tensor.matmul(ps_b1[:], ones_row_f[:], rsrow[:],
                                         start=True, stop=True).then_inc(
                            sems["pe"], 1)
                        pv = sems.inc("pe")
                        nc.vector.wait_ge(sems["pe"], pv)
                        nc.vector.tensor_copy(out=nb_b[:], in_=ps_b0[:])
                        nc.vector.tensor_copy(out=rs_b[:], in_=ps_b1[:]).then_inc(
                            sems["dve"], 1)
                        sems.inc("dve")
                        for et in range(ECH):
                            guard(nc.sync, "dve")
                            lv = dma(nc.sync, xm_tile[:],
                                     xmid_d[128 * et:128 * et + 128,
                                            512 * half:512 * half + 512], "ld")
                            nc.vector.wait_ge(sems["ld"], lv)
                            guard(nc.vector, "sv")
                            nc.vector.tensor_add(out=sqst[:], in0=xm_tile[:],
                                                 in1=nb_b[:])
                            nc.vector.tensor_mul(out=xn2st[:], in0=sqst[:],
                                                 in1=rs_b[:]).then_inc(
                                sems["dve"], 1)
                            dv = sems.inc("dve")
                            nc.sync.wait_ge(sems["dve"], dv)
                            dma(nc.sync,
                                xn2_d[128 * et:128 * et + 128,
                                      512 * half:512 * half + 512],
                                xn2st[:], "sv")
                nc.sync.wait_ge(sems["sv"], sems.val("sv"))
            nc.all_engine_barrier()

        if "C1" in phases:
            with ExitStack() as es:
                indgu = es.enter_context(nc.sbuf_tensor("indgu", [32, 4 * 128], BF16))
                qwg_sb = es.enter_context(nc.sbuf_tensor("qwg_sb", [128, 4 * 512], I32))
                qwu_sb = es.enter_context(nc.sbuf_tensor("qwu_sb", [128, 4 * 512], I32))
                scg_b = es.enter_context(nc.sbuf_tensor("scg_b", [32, 512], BF16))
                scu_b = es.enter_context(nc.sbuf_tensor("scu_b", [32, 512], BF16))
                qzg_sb = es.enter_context(nc.sbuf_tensor("qzg_sb", [32, 64], I32))
                qzu_sb = es.enter_context(nc.sbuf_tensor("qzu_sb", [32, 64], I32))
                z4g_i = es.enter_context(nc.sbuf_tensor("z4g_i", [32, 512], I32))
                z4u_i = es.enter_context(nc.sbuf_tensor("z4u_i", [32, 512], I32))
                z4g_b = es.enter_context(nc.sbuf_tensor("z4g_b", [32, 512], BF16))
                z4u_b = es.enter_context(nc.sbuf_tensor("z4u_b", [32, 512], BF16))
                sbg = es.enter_context(nc.sbuf_tensor("sbg", [128, 512], BF16))
                zbg = es.enter_context(nc.sbuf_tensor("zbg", [128, 512], BF16))
                sbu = es.enter_context(nc.sbuf_tensor("sbu", [128, 512], BF16))
                zbu = es.enter_context(nc.sbuf_tensor("zbu", [128, 512], BF16))
                ui32 = es.enter_context(nc.sbuf_tensor("ui32", [128, 512], I32))
                wmz = es.enter_context(nc.sbuf_tensor("wmz", [128, 512], BF16))
                sg_b = es.enter_context(nc.sbuf_tensor("sg_b", [128, 512], BF16))
                hstage = es.enter_context(nc.sbuf_tensor("hstage", [128, 512], BF16))
                ps_bc = es.enter_context(nc.psum_tensor("ps_bc", [128, 512], F32))
                ps_g = es.enter_context(nc.psum_tensor("ps_g", [128, 512], F32))
                ps_u = es.enter_context(nc.psum_tensor("ps_u", [128, 512], F32))
                xn2t_v = big1[:].rearrange("p (c m) -> p c m", c=ECH)
                wg_v = big2[:, 0:16384].rearrange("p (c n) -> p c n", c=ECH)
                wu_v = big2[:, 16384:32768].rearrange("p (c n) -> p c n", c=ECH)
                xn2_pl = xn2_d.rearrange("(r j) m -> j r m", j=8)
                for j in range(8):
                    for R in range(4):
                        c = 4 * j + R
                        dma(nc.sync, xn2t_v[:, c, :],
                            xn2_pl[j, 128 * R:128 * R + 128, :], "ld")
                dma(nc.gpsimd, indgu[:], indtgu_in, "ldg")
                nc.sync.wait_ge(sems["ld"], sems.val("ld"))
                nc.gpsimd.wait_ge(sems["ldg"], sems.val("ldg"))
                nc.all_engine_barrier()

                qwg_v = qwg_sb[:].rearrange("p (r n) -> p r n", r=4)
                qwu_v = qwu_sb[:].rearrange("p (r n) -> p r n", r=4)

                for p in range(NPAN):
                    c0 = 512 * p
                    W = min(512, INTER - c0)
                    guard(nc.sync, "dve", "pool", "pe")
                    for R in range(4):
                        dma(nc.sync, qwg_v[:, R, 0:W],
                            qw_g[128 * R:128 * R + 128, c0:c0 + W], "ld")
                        dma(nc.sync, qwu_v[:, R, 0:W],
                            qw_u[128 * R:128 * R + 128, c0:c0 + W], "ld")
                    dma(nc.gpsimd, scg_b[:, 0:W], sc_g[:, c0:c0 + W], "ldg")
                    dma(nc.gpsimd, scu_b[:, 0:W], sc_u[:, c0:c0 + W], "ldg")
                    dma(nc.sync, qzg_sb[:, 0:W // 8],
                        qz_g[:, c0 // 8:c0 // 8 + W // 8], "ld")
                    lv = dma(nc.sync, qzu_sb[:, 0:W // 8],
                             qz_u[:, c0 // 8:c0 // 8 + W // 8], "ld")
                    nc.vector.wait_ge(sems["ld"], lv)
                    nc.vector.wait_ge(sems["ldg"], sems.val("ldg"))
                    for j in range(8):
                        nc.vector.tensor_scalar(
                            out=z4g_i[:, j:W:8], in0=qzg_sb[:, 0:W // 8],
                            scalar1=4 * j, scalar2=15,
                            op0=ALU.logical_shift_right, op1=ALU.bitwise_and)
                        nc.vector.tensor_scalar(
                            out=z4u_i[:, j:W:8], in0=qzu_sb[:, 0:W // 8],
                            scalar1=4 * j, scalar2=15,
                            op0=ALU.logical_shift_right, op1=ALU.bitwise_and)
                    nc.vector.tensor_scalar(out=z4g_b[:, 0:W], in0=z4g_i[:, 0:W],
                                            scalar1=1.0, scalar2=None, op0=ALU.add)
                    nc.vector.tensor_scalar(out=z4u_b[:, 0:W], in0=z4u_i[:, 0:W],
                                            scalar1=1.0, scalar2=None,
                                            op0=ALU.add).then_inc(sems["dve"], 1)
                    dv_z = sems.inc("dve")
                    nc.tensor.wait_ge(sems["dve"], dv_z)
                    for R in range(4):
                        for (src_b, dst_b) in ((scg_b, sbg), (z4g_b, zbg),
                                               (scu_b, sbu), (z4u_b, zbu)):
                            nc.tensor.matmul(
                                ps_bc[:, 0:W], indgu[:, 128 * R:128 * R + 128],
                                src_b[:, 0:W], start=True,
                                stop=True).then_inc(sems["pe"], 1)
                            pv = sems.inc("pe")
                            nc.vector.wait_ge(sems["pe"], pv)
                            guard(nc.vector, "pool")
                            nc.vector.tensor_copy(out=dst_b[:, 0:W],
                                                  in_=ps_bc[:, 0:W]).then_inc(
                                sems["dve"], 1)
                            dv = sems.inc("dve")
                            nc.tensor.wait_ge(sems["dve"], dv)
                        for j in range(8):
                            c = 4 * j + R
                            for (qv, zb, sb_, wv) in ((qwg_v, zbg, sbg, wg_v),
                                                      (qwu_v, zbu, sbu, wu_v)):
                                guard(nc.vector, "pool")
                                nc.vector.tensor_scalar(
                                    out=ui32[:, 0:W], in0=qv[:, R, 0:W],
                                    scalar1=4 * j, scalar2=15,
                                    op0=ALU.logical_shift_right,
                                    op1=ALU.bitwise_and)
                                nc.vector.tensor_sub(
                                    out=wmz[:, 0:W], in0=ui32[:, 0:W],
                                    in1=zb[:, 0:W]).then_inc(sems["dve"], 1)
                                dv = sems.inc("dve")
                                nc.gpsimd.wait_ge(sems["dve"], dv)
                                guard(nc.gpsimd, "pe")
                                nc.gpsimd.tensor_mul(
                                    out=wv[:, c, 0:W], in0=wmz[:, 0:W],
                                    in1=sb_[:, 0:W]).then_inc(sems["pool"], 1)
                                sems.inc("pool")
                    pool_done = sems.val("pool")
                    nc.tensor.wait_ge(sems["pool"], pool_done)
                    for tt in range(W // 128):
                        it = 4 * p + tt
                        for half in (0, 1):
                            guard(nc.tensor, "act", "dve")
                            for c in range(ECH):
                                nc.tensor.matmul(
                                    ps_g[:], wg_v[:, c, 128 * tt:128 * tt + 128],
                                    xn2t_v[:, c, 512 * half:512 * half + 512],
                                    start=(c == 0), stop=(c == ECH - 1))
                            for c in range(ECH):
                                _li = nc.tensor.matmul(
                                    ps_u[:], wu_v[:, c, 128 * tt:128 * tt + 128],
                                    xn2t_v[:, c, 512 * half:512 * half + 512],
                                    start=(c == 0), stop=(c == ECH - 1))
                            _li.then_inc(sems["pe"], 1)
                            pv = sems.inc("pe")
                            nc.scalar.wait_ge(sems["pe"], pv)
                            guard(nc.scalar, "dve")
                            nc.scalar.activation(out=sg_b[:], in_=ps_g[:],
                                                 func=AF.Silu).then_inc(
                                sems["act"], 1)
                            av = sems.inc("act")
                            nc.vector.wait_ge(sems["act"], av)
                            guard(nc.vector, "sv")
                            nc.vector.tensor_mul(out=hstage[:], in0=sg_b[:],
                                                 in1=ps_u[:]).then_inc(
                                sems["dve"], 1)
                            dv = sems.inc("dve")
                            nc.sync.wait_ge(sems["dve"], dv)
                            dma(nc.sync,
                                ht_d[128 * it:128 * it + 128,
                                     512 * half:512 * half + 512],
                                hstage[:], "sv")
                nc.sync.wait_ge(sems["sv"], sems.val("sv"))
            nc.all_engine_barrier()

        if "C2" in phases:
            with ExitStack() as es:
                indd = es.enter_context(nc.sbuf_tensor("indd", [86, RTD * 128], BF16))
                scd_b = es.enter_context(nc.sbuf_tensor("scd_b", [86, 512], BF16))
                qzd_sb = es.enter_context(nc.sbuf_tensor("qzd_sb", [86, 64], I32))
                z4d_i = es.enter_context(nc.sbuf_tensor("z4d_i", [86, 512], I32))
                z4d_b = es.enter_context(nc.sbuf_tensor("z4d_b", [86, 512], BF16))
                sbd = es.enter_context(nc.sbuf_tensor("sbd", [128, RTD * 512], BF16))
                zbd = es.enter_context(nc.sbuf_tensor("zbd", [128, RTD * 512], BF16))
                qwd_sb = es.enter_context(nc.sbuf_tensor("qwd_sb", [128, 512], I32))
                ud32 = es.enter_context(nc.sbuf_tensor("ud32", [128, 512], I32))
                wmzd = es.enter_context(nc.sbuf_tensor("wmzd", [128, 512], BF16))
                wdd = es.enter_context(nc.sbuf_tensor("wdd", [128, 512], BF16))
                htc = es.enter_context(nc.sbuf_tensor("htc", [128, 1024], BF16))
                xmst = es.enter_context(nc.sbuf_tensor("xmst", [128, 512], F32))
                fo8 = es.enter_context(nc.sbuf_tensor("fo8", [128, 4096], F32))
                natst = es.enter_context(nc.sbuf_tensor("natst", [128, 512], F32))
                dma(nc.gpsimd, indd[:], indtd_in, "ldg")
                nc.gpsimd.wait_ge(sems["ldg"], sems.val("ldg"))
                sbd_v = sbd[:].rearrange("p (r n) -> p r n", r=RTD)
                zbd_v = zbd[:].rearrange("p (r n) -> p r n", r=RTD)
                ht_pl = ht_d.rearrange("(r j) m -> j r m", j=8)
                for eg in range(8):
                    e0 = 512 * eg
                    guard(nc.sync, "pe", "dve")
                    dma(nc.gpsimd, scd_b[:], sc_d[:, e0:e0 + 512], "ldg")
                    lv = dma(nc.sync, qzd_sb[:], qz_d[:, e0 // 8:e0 // 8 + 64],
                             "ld")
                    nc.vector.wait_ge(sems["ld"], lv)
                    nc.vector.wait_ge(sems["ldg"], sems.val("ldg"))
                    for j in range(8):
                        nc.vector.tensor_scalar(
                            out=z4d_i[:, j::8], in0=qzd_sb[:], scalar1=4 * j,
                            scalar2=15, op0=ALU.logical_shift_right,
                            op1=ALU.bitwise_and)
                    nc.vector.tensor_scalar(out=z4d_b[:], in0=z4d_i[:],
                                            scalar1=1.0, scalar2=None,
                                            op0=ALU.add).then_inc(sems["dve"], 1)
                    dv = sems.inc("dve")
                    nc.tensor.wait_ge(sems["dve"], dv)
                    with nc.psum_tensor(f"ps_pre{eg}", [128, 512], F32) as ps_pre:
                        for rt in range(RTD):
                            for (src_b, dstv) in ((scd_b, sbd_v), (z4d_b, zbd_v)):
                                nc.tensor.matmul(
                                    ps_pre[:], indd[:, 128 * rt:128 * rt + 128],
                                    src_b[:], start=True,
                                    stop=True).then_inc(sems["pe"], 1)
                                pv = sems.inc("pe")
                                nc.vector.wait_ge(sems["pe"], pv)
                                nc.vector.tensor_copy(
                                    out=dstv[:, rt, :], in_=ps_pre[:]).then_inc(
                                    sems["dve"], 1)
                                dv = sems.inc("dve")
                                nc.tensor.wait_ge(sems["dve"], dv)
                    with nc.psum_tensor(f"ps_d8{eg}", [128, 4096], F32) as ps_d8:
                        for rt in range(RTD):
                            cnt = 128 if rt < 10 else RTD_TAIL
                            guard(nc.sync, "dve")
                            lq = dma(nc.sync, qwd_sb[0:cnt, :],
                                     qw_d[128 * rt:128 * rt + cnt, e0:e0 + 512],
                                     "ld")
                            for j in range(8):
                                first = (rt == 0 and j == 0)
                                last = (rt == RTD - 1 and j == 7)
                                guard(nc.sync, "pe")
                                lv = dma(nc.sync, htc[0:cnt, :],
                                         ht_pl[j, 128 * rt:128 * rt + cnt, :],
                                         "ld")
                                nc.vector.wait_ge(sems["ld"], max(lq, lv))
                                guard(nc.vector, "pool")
                                nc.vector.tensor_scalar(
                                    out=ud32[0:cnt, :], in0=qwd_sb[0:cnt, :],
                                    scalar1=4 * j, scalar2=15,
                                    op0=ALU.logical_shift_right,
                                    op1=ALU.bitwise_and)
                                nc.vector.tensor_sub(
                                    out=wmzd[0:cnt, :], in0=ud32[0:cnt, :],
                                    in1=zbd_v[0:cnt, rt, :]).then_inc(
                                    sems["dve"], 1)
                                dv = sems.inc("dve")
                                nc.gpsimd.wait_ge(sems["dve"], dv)
                                guard(nc.gpsimd, "pe")
                                nc.gpsimd.tensor_mul(
                                    out=wdd[0:cnt, :], in0=wmzd[0:cnt, :],
                                    in1=sbd_v[0:cnt, rt, :]).then_inc(
                                    sems["pool"], 1)
                                pl = sems.inc("pool")
                                nc.tensor.wait_ge(sems["pool"], pl)
                                for el in range(4):
                                    for half in (0, 1):
                                        off = 512 * (2 * el + half)
                                        _li = nc.tensor.matmul(
                                            ps_d8[:, off:off + 512],
                                            wdd[0:cnt,
                                                128 * el:128 * el + 128],
                                            htc[0:cnt,
                                                512 * half:512 * half + 512],
                                            start=first, stop=last)
                                _li.then_inc(sems["pe"], 1)
                                sems.inc("pe")
                        for el in range(4):
                            et = 4 * eg + el
                            for half in (0, 1):
                                off = 512 * (2 * el + half)
                                guard(nc.sync, "dve")
                                lv = dma(nc.sync, xmst[:],
                                         xmid_d[128 * et:128 * et + 128,
                                                512 * half:512 * half + 512],
                                         "ld")
                                nc.vector.wait_ge(sems["ld"], lv)
                                guard(nc.vector, "pe")
                                nc.vector.tensor_add(
                                    out=fo8[:, off:off + 512],
                                    in0=ps_d8[:, off:off + 512],
                                    in1=xmst[:]).then_inc(sems["dve"], 1)
                                sems.inc("dve")
                    with nc.psum_tensor(f"ps_tr{eg}", [128, 512], F32) as ps_tr:
                        for el in range(4):
                            et = 4 * eg + el
                            for half in (0, 1):
                                off = 512 * (2 * el + half)
                                guard(nc.tensor, "dve")
                                for qq in range(4):
                                    _li = nc.tensor.transpose(
                                        ps_tr[:, 128 * qq:128 * qq + 128],
                                        fo8[:, off + 128 * qq:off + 128 * qq + 128],
                                        identf[:])
                                _li.then_inc(sems["pe"], 1)
                                pv = sems.inc("pe")
                                nc.vector.wait_ge(sems["pe"], pv)
                                guard(nc.vector, "sv")
                                nc.vector.tensor_copy(
                                    out=natst[:], in_=ps_tr[:]).then_inc(
                                    sems["dve"], 1)
                                dv = sems.inc("dve")
                                nc.sync.wait_ge(sems["dve"], dv)
                                for qq in range(4):
                                    dma(nc.sync,
                                        out[512 * half + 128 * qq:
                                            512 * half + 128 * qq + 128,
                                            128 * et:128 * et + 128],
                                        natst[:, 128 * qq:128 * qq + 128],
                                        "sv")
            nc.sync.wait_ge(sems["sv"], sems.val("sv"))
            nc.all_engine_barrier()

    return nc


def _in_maps(inputs):
    cosT, sinT, ident, RT, indt_gu, indt_d = _consts()
    MASK_LIST, mask_tiles = _masks()
    x = np.ascontiguousarray(
        np.asarray(inputs["L0_attn_hidden_states"], dtype=np.float32).reshape(S, E))
    shared = {
        "wq": np.ascontiguousarray(np.asarray(inputs["L0_attn_Wq"], np.float32)),
        "wk": np.ascontiguousarray(np.asarray(inputs["L0_attn_Wk"], np.float32)),
        "wv": np.ascontiguousarray(np.asarray(inputs["L0_attn_Wv"], np.float32)),
        "wo": np.ascontiguousarray(np.asarray(inputs["L0_attn_Wo"], np.float32)),
        "qw_g": np.ascontiguousarray(np.asarray(inputs["L0_ffn_gate_qweight"], np.int32)),
        "sc_g": np.ascontiguousarray(np.asarray(inputs["L0_ffn_gate_scales"], np.float32)),
        "qz_g": np.ascontiguousarray(np.asarray(inputs["L0_ffn_gate_qzeros"], np.int32)),
        "qw_u": np.ascontiguousarray(np.asarray(inputs["L0_ffn_up_qweight"], np.int32)),
        "sc_u": np.ascontiguousarray(np.asarray(inputs["L0_ffn_up_scales"], np.float32)),
        "qz_u": np.ascontiguousarray(np.asarray(inputs["L0_ffn_up_qzeros"], np.int32)),
        "qw_d": np.ascontiguousarray(np.asarray(inputs["L0_ffn_down_qweight"], np.int32)),
        "sc_d": np.ascontiguousarray(np.asarray(inputs["L0_ffn_down_scales"], np.float32)),
        "qz_d": np.ascontiguousarray(np.asarray(inputs["L0_ffn_down_qzeros"], np.int32)),
        "ident": ident,
        "rotm": RT,
        "indt_gu": indt_gu,
        "indt_d": indt_d,
    }
    maps = []
    for c in range(NCORES):
        m = dict(shared)
        m["x_rot"] = np.ascontiguousarray(np.roll(x, -BLK * c, axis=0))
        m["cosT"] = np.ascontiguousarray(np.roll(cosT, -BLK * c, axis=1))
        m["sinT"] = np.ascontiguousarray(np.roll(sinT, -BLK * c, axis=1))
        nm = max(len(MASK_LIST), 1)
        mk = np.zeros((128, nm * 512), np.float32)
        for i in range(len(MASK_LIST)):
            mk[:, 512 * i:512 * i + 512] = mask_tiles[c][i]
        m["masks"] = mk
        maps.append(m)
    return maps


def kernel(**inputs):
    nc = build_program()
    maps = _in_maps(inputs)
    res = run_bass_kernel_spmd(nc, maps, core_ids=list(range(NCORES)))
    out = np.concatenate([res.results[c]["out"] for c in range(NCORES)], axis=0)
    return out.reshape(1, S, E).astype(np.float32)


# version 03
# speedup vs baseline: 1.5349x; 1.5349x over previous
import numpy as np
import ml_dtypes
import concourse.bass as bass
import concourse.mybir as mybir
from concourse.bass_utils import run_bass_kernel_spmd
from contextlib import ExitStack

F32 = mybir.dt.float32
BF16 = mybir.dt.bfloat16
I32 = mybir.dt.int32
AF = mybir.ActivationFunctionType
ALU = mybir.AluOpType
BF = ml_dtypes.bfloat16

S, E, H, KVH, D = 8192, 4096, 32, 8, 128
NB, BLK = 8, 1024
INTER = 11008
NCORES = 8
SCALE = float(1.0 / np.sqrt(128.0))
EXP_SHIFT = 12.0
EPS = 1e-5
WINDOW = 1024

ECH = E // 128
NPAN = (INTER + 511) // 512
RTD = 11
RTD_TAIL = INTER // 8 - 10 * 128
NVB = 56

_PHASES = ("W", "A", "B1", "B2", "B3", "B4", "C1", "C2")


def _mask_tiles():
    mask_list = []
    tiles = [[] for _ in range(NCORES)]
    k_idx = np.arange(128)[:, None]
    q_idx = np.arange(512)[None, :]
    for Bb in range(8, 64):
        for half in (0, 1):
            core_tiles = []
            any_zero = False
            for c in range(NCORES):
                j_g = (128 * Bb + k_idx + BLK * c) % S
                i_g = BLK * c + 512 * half + q_idx
                keep = (np.abs(i_g - j_g) >= WINDOW).astype(np.float32)
                core_tiles.append(keep)
                any_zero |= bool((keep == 0).any())
            if any_zero:
                mask_list.append((Bb, half))
                for c in range(NCORES):
                    tiles[c].append(core_tiles[c])
    return mask_list, [np.stack(t, axis=0) for t in tiles]


_CACHE = {}


def _masks():
    if "m" not in _CACHE:
        _CACHE["m"] = _mask_tiles()
    return _CACHE["m"]


def _consts():
    if "c" in _CACHE:
        return _CACHE["c"]
    inv_freq = 1.0 / (10000.0 ** (np.arange(0, D, 2, dtype=np.float64) / D))
    t = np.arange(S, dtype=np.float64)
    freqs = t[:, None] * inv_freq[None, :]
    emb = np.concatenate([freqs, freqs], axis=1)
    cosT = np.cos(emb).astype(np.float32).T.copy()
    sinT = np.sin(emb).astype(np.float32).T.copy()
    ident = np.eye(128, dtype=np.float32)
    RT = np.zeros((128, 128), dtype=np.float32)
    for d in range(64):
        RT[d + 64, d] = -1.0
        RT[d, d + 64] = 1.0
    indt_gu = np.zeros((32, 4 * 128), dtype=np.float32)
    for R in range(4):
        for p in range(128):
            indt_gu[8 * R + p // 16, 128 * R + p] = 1.0
    indt_d = np.zeros((86, RTD * 128), dtype=np.float32)
    for rt in range(RTD):
        cnt = 128 if rt < 10 else RTD_TAIL
        for p in range(cnt):
            indt_d[8 * rt + p // 16, 128 * rt + p] = 1.0
    _CACHE["c"] = (cosT, sinT, ident, RT, indt_gu, indt_d)
    return _CACHE["c"]


class Sems:
    def __init__(self, nc, ctx, names):
        self.h, self.c = {}, {}
        for n in names:
            self.h[n] = ctx.enter_context(nc.semaphore(n))
            self.c[n] = 0

    def __getitem__(self, n):
        return self.h[n]

    def inc(self, n, v=1):
        self.c[n] += v
        return self.c[n]

    def val(self, n):
        return self.c[n]


def build_program(upto="C2", dbg=False):
    nphase = _PHASES.index(upto) + 1
    phases = set(_PHASES[:nphase])
    nc = bass.Bass("TRN2", target_bir_lowering=False, debug=False)
    MASK_LIST, _ = _masks()
    NMASK = len(MASK_LIST)
    MASK_IDX = {bh: i for i, bh in enumerate(MASK_LIST)}

    scratch_kind = "ExternalOutput" if dbg else "Internal"

    x_in = nc.dram_tensor("x_rot", [S, E], F32, kind="ExternalInput").ap()
    wq = nc.dram_tensor("wq", [H * D, E], F32, kind="ExternalInput").ap()
    wk = nc.dram_tensor("wk", [KVH * D, E], F32, kind="ExternalInput").ap()
    wv = nc.dram_tensor("wv", [KVH * D, E], F32, kind="ExternalInput").ap()
    wo = nc.dram_tensor("wo", [E, H * D], F32, kind="ExternalInput").ap()
    qw_g = nc.dram_tensor("qw_g", [E // 8, INTER], I32, kind="ExternalInput").ap()
    sc_g = nc.dram_tensor("sc_g", [32, INTER], F32, kind="ExternalInput").ap()
    qz_g = nc.dram_tensor("qz_g", [32, INTER // 8], I32, kind="ExternalInput").ap()
    qw_u = nc.dram_tensor("qw_u", [E // 8, INTER], I32, kind="ExternalInput").ap()
    sc_u = nc.dram_tensor("sc_u", [32, INTER], F32, kind="ExternalInput").ap()
    qz_u = nc.dram_tensor("qz_u", [32, INTER // 8], I32, kind="ExternalInput").ap()
    qw_d = nc.dram_tensor("qw_d", [INTER // 8, E], I32, kind="ExternalInput").ap()
    sc_d = nc.dram_tensor("sc_d", [86, E], F32, kind="ExternalInput").ap()
    qz_d = nc.dram_tensor("qz_d", [86, E // 8], I32, kind="ExternalInput").ap()
    cos_in = nc.dram_tensor("cosT", [128, S], F32, kind="ExternalInput").ap()
    sin_in = nc.dram_tensor("sinT", [128, S], F32, kind="ExternalInput").ap()
    ident_in = nc.dram_tensor("ident", [128, 128], F32, kind="ExternalInput").ap()
    rot_in = nc.dram_tensor("rotm", [128, 128], F32, kind="ExternalInput").ap()
    indtgu_in = nc.dram_tensor("indt_gu", [32, 4 * 128], F32,
                               kind="ExternalInput").ap()
    indtd_in = nc.dram_tensor("indt_d", [86, RTD * 128], F32,
                              kind="ExternalInput").ap()
    masks_in = nc.dram_tensor("masks", [128, max(NMASK, 1) * 512], F32,
                              kind="ExternalInput").ap()
    out = nc.dram_tensor("out", [BLK, E], F32, kind="ExternalOutput").ap()

    wqt_d = nc.dram_tensor("wqt_d", [H * D // 128, 128, ECH, 128], BF16,
                           kind=scratch_kind).ap()
    wkt_d = nc.dram_tensor("wkt_d", [KVH * D // 128, 128, ECH, 128], BF16,
                           kind=scratch_kind).ap()
    wvt_d = nc.dram_tensor("wvt_d", [KVH * D // 128, 128, ECH, 128], BF16,
                           kind=scratch_kind).ap()
    wot_d = nc.dram_tensor("wot_d", [E // 128, 128, H * D // 128, 128], BF16,
                           kind=scratch_kind).ap()
    kt_d = nc.dram_tensor("kt_d", [KVH * D, S], BF16, kind=scratch_kind).ap()
    v_d = nc.dram_tensor("v_d", [KVH, S // 128, 128, 128], BF16,
                         kind=scratch_kind).ap()
    xt_d = nc.dram_tensor("xt_d", [E, BLK], F32, kind=scratch_kind).ap()
    qt_dbg = (nc.dram_tensor("qt_dbg", [H * D, BLK], BF16, kind="ExternalOutput").ap()
              if dbg else None)
    xnt_dbg = (nc.dram_tensor("xnt_dbg", [E, BLK], BF16, kind="ExternalOutput").ap()
               if dbg else None)
    aot_dbg = (nc.dram_tensor("aot_dbg", [H * D, BLK], BF16,
                              kind="ExternalOutput").ap() if dbg else None)
    xmid_d = nc.dram_tensor("xmid_d", [E, BLK], F32, kind=scratch_kind).ap()
    xn2_d = nc.dram_tensor("xn2_d", [E, BLK], BF16, kind=scratch_kind).ap()
    ht_d = nc.dram_tensor("ht_d", [INTER, BLK], BF16, kind=scratch_kind).ap()

    ctx = ExitStack()
    with ctx:
        sems = Sems(nc, ctx, ["ld", "ldg", "sv", "dve", "act", "pe", "pool"])

        def dma(eng, out_, in_, sem="ld", **kw):
            eng.dma_start(out=out_, in_=in_, **kw).then_inc(sems[sem], 16)
            return sems.inc(sem, 16)

        def guard(eng, *names):
            for n in names:
                if sems.val(n) > 0:
                    eng.wait_ge(sems[n], sems.val(n))

        sb = lambda n, s_, d_: ctx.enter_context(nc.sbuf_tensor(n, s_, d_))

        identb = sb("identb", [128, 128], BF16)
        identf = sb("identf", [128, 128], F32)
        rotf = sb("rotf", [128, 128], F32)
        ones_col_b = sb("ones_col_b", [128, 1], BF16)
        ones_row_f = sb("ones_row_f", [1, 128], F32)
        ones_col_f = sb("ones_col_f", [128, 1], F32)
        ebias = sb("ebias", [128, 1], F32)
        big1 = sb("big1", [128, 32768], BF16)
        big2 = sb("big2", [128, 32768], BF16)

        dma(nc.sync, identf[:], ident_in)
        dma(nc.sync, rotf[:], rot_in)
        dma(nc.gpsimd, identb[:], ident_in, "ldg")
        nc.gpsimd.wait_ge(sems["ldg"], sems.val("ldg"))
        nc.gpsimd.memset(ones_col_b[:], 1.0)
        nc.gpsimd.memset(ones_row_f[:], 1.0)
        nc.gpsimd.memset(ones_col_f[:], 1.0)
        nc.gpsimd.memset(ebias[:], -EXP_SHIFT)
        nc.sync.wait_ge(sems["ld"], sems.val("ld"))
        nc.all_engine_barrier()

        if "W" in phases:
            with (
                nc.sbuf_tensor("w_nat", [128, 2 * E], BF16) as w_nat,
                nc.sbuf_tensor("w_stage", [128, 2 * 512], BF16) as w_stage,
                nc.psum_tensor("w_ps", [128, 2 * 1024], BF16) as w_ps,
            ):
                wt_state = {"t": 0, "dve_mark": {}, "pe_mark": {}, "sv_mark": {},
                            "ld_mark": {}}

                def transpose_weight(w_src, wt_dst, n_tiles, n_free):
                    st = wt_state
                    nfc = n_free // 128
                    for tl in range(n_tiles):
                        t_g = st["t"]
                        st["t"] += 1
                        tp = t_g % 2
                        if t_g >= 2:
                            nc.gpsimd.wait_ge(sems["pe"], st["pe_mark"][t_g - 2])
                        st["ld_mark"][t_g] = dma(
                            nc.gpsimd, w_nat[:, E * tp:E * tp + n_free],
                            w_src[128 * tl:128 * tl + 128, :], "ldg")
                        nc.tensor.wait_ge(sems["ldg"], st["ld_mark"][t_g])
                        for cg in range(nfc // 4):
                            gn = t_g * (nfc // 4) + cg
                            gp = gn % 2
                            if gn >= 2 and (gn - 2) in st["dve_mark"]:
                                nc.tensor.wait_ge(sems["dve"],
                                                  st["dve_mark"][gn - 2])
                            for i in range(4):
                                c = 4 * cg + i
                                _li = nc.tensor.transpose(
                                    w_ps[:, 1024 * gp + 128 * i:
                                         1024 * gp + 128 * i + 128],
                                    w_nat[:, E * tp + 128 * c:
                                          E * tp + 128 * c + 128], identb[:])
                            _li.then_inc(sems["pe"], 1)
                            pv = sems.inc("pe")
                            if cg == nfc // 4 - 1:
                                st["pe_mark"][t_g] = pv
                            nc.vector.wait_ge(sems["pe"], pv)
                            if gn >= 2 and (gn - 2) in st["sv_mark"]:
                                nc.vector.wait_ge(sems["sv"],
                                                  st["sv_mark"][gn - 2])
                            nc.vector.tensor_copy(
                                out=w_stage[:, 512 * gp:512 * gp + 512],
                                in_=w_ps[:, 1024 * gp:1024 * gp + 512]).then_inc(
                                sems["dve"], 1)
                            st["dve_mark"][gn] = sems.inc("dve")
                            nc.sync.wait_ge(sems["dve"], st["dve_mark"][gn])
                            dst = wt_dst[tl, :, 4 * cg:4 * cg + 4, :]
                            src = w_stage[:, 512 * gp:512 * gp + 512].rearrange(
                                "p (cc n) -> p cc n", cc=4)
                            st["sv_mark"][gn] = dma(nc.sync, dst, src, "sv")

                transpose_weight(wq, wqt_d, H * D // 128, E)
                transpose_weight(wk, wkt_d, KVH * D // 128, E)
                transpose_weight(wv, wvt_d, KVH * D // 128, E)
                transpose_weight(wo, wot_d, E // 128, H * D)
                nc.sync.wait_ge(sems["sv"], sems.val("sv"))
            nc.all_engine_barrier()

        with ExitStack() as es:
            x_sub = es.enter_context(nc.sbuf_tensor("x_sub", [128, E], F32))
            xn_sub = es.enter_context(nc.sbuf_tensor("xn_sub", [128, E], BF16))
            st6 = es.enter_context(nc.sbuf_tensor("st6", [128, 48], F32))
            st2 = es.enter_context(nc.sbuf_tensor("st2", [128, 2], F32))
            negmu = es.enter_context(nc.sbuf_tensor("negmu", [128, 1], F32))
            spacer = es.enter_context(nc.sbuf_tensor("spacer", [128, 1], F32))
            rstd = es.enter_context(nc.sbuf_tensor("rstd", [128, 1], F32))
            wpan = es.enter_context(nc.sbuf_tensor("wpan", [128, 2 * ECH * 128], BF16))
            wpan2 = es.enter_context(nc.sbuf_tensor("wpan2", [128, 2 * ECH * 128], BF16))
            cosb = es.enter_context(nc.sbuf_tensor("cosb", [128, BLK], F32))
            sinb = es.enter_context(nc.sbuf_tensor("sinb", [128, BLK], F32))
            kf = es.enter_context(nc.sbuf_tensor("kf", [128, 512], F32))
            t1f = es.enter_context(nc.sbuf_tensor("t1f", [128, 512], F32))
            t2f = es.enter_context(nc.sbuf_tensor("t2f", [128, 512], F32))
            kvb = es.enter_context(nc.sbuf_tensor("kvb", [128, 512], BF16))
            vb_c = es.enter_context(nc.sbuf_tensor("vb_c", [128, 512], BF16))
            vstage = es.enter_context(nc.sbuf_tensor("vstage", [128, 512], BF16))
            xt_stage = es.enter_context(nc.sbuf_tensor("xt_stage", [128, 512], F32))
            ps_t = es.enter_context(nc.psum_tensor("ps_t", [128, 512], BF16))
            ps_tf = es.enter_context(nc.psum_tensor("ps_tf", [128, 512], F32))
            ps_k = es.enter_context(nc.psum_tensor("ps_k", [128, 512], F32))
            ps_v = es.enter_context(nc.psum_tensor("ps_v", [128, 512], F32))
            ps_r = es.enter_context(nc.psum_tensor("ps_r", [128, 512], F32))
            xnt_v = big1[:].rearrange("p (c m) -> p c m", c=ECH)
            qt_v = big2[:].rearrange("p (c m) -> p c m", c=ECH)
            wpan_v = wpan[:].rearrange("p (b c n) -> p b c n", b=2, c=ECH)
            wpan2_v = wpan2[:].rearrange("p (b c n) -> p b c n", b=2, c=ECH)

            def ln_block(b, also_xt):
                for ms in range(8):
                    row0 = BLK * b + 128 * ms
                    guard(nc.gpsimd, "pe")
                    lv = dma(nc.gpsimd, x_sub[:], x_in[row0:row0 + 128, :], "ldg")
                    nc.vector.wait_ge(sems["ldg"], lv)
                    for i in range(8):
                        nc.vector.bn_stats(out=st6[:, 6 * i:6 * i + 6],
                                           in_=x_sub[:, 512 * i:512 * i + 512])
                    nc.vector.tensor_copy(out=spacer[:], in_=ebias[:])
                    nc.vector.tensor_copy(out=spacer[:], in_=ebias[:])
                    nc.vector.bn_aggr(out=st2[:], in_=st6[:])
                    nc.vector.tensor_copy(out=spacer[:], in_=ebias[:])
                    nc.vector.tensor_copy(out=spacer[:], in_=ebias[:])
                    nc.vector.tensor_scalar(out=negmu[:], in0=st2[:, 0:1],
                                            scalar1=-1.0, scalar2=None, op0=ALU.mult)
                    nc.vector.tensor_scalar(
                        out=rstd[:], in0=st2[:, 1:2], scalar1=EPS, scalar2=None,
                        op0=ALU.add).then_inc(sems["dve"], 1)
                    dv = sems.inc("dve")
                    nc.scalar.wait_ge(sems["dve"], dv)
                    nc.scalar.activation(out=rstd[:], in_=rstd[:], func=AF.Ln)
                    nc.scalar.activation(out=rstd[:], in_=rstd[:], func=AF.Exp,
                                         scale=-0.5).then_inc(sems["act"], 1)
                    av = sems.inc("act")
                    nc.vector.wait_ge(sems["act"], av)
                    guard(nc.vector, "pe")
                    nc.vector.tensor_scalar(
                        out=xn_sub[:], in0=x_sub[:], scalar1=negmu[:, 0:1],
                        scalar2=rstd[:, 0:1], op0=ALU.add,
                        op1=ALU.mult).then_inc(sems["dve"], 1)
                    dv = sems.inc("dve")
                    nc.tensor.wait_ge(sems["dve"], dv)
                    for cg in range(8):
                        guard(nc.tensor, "dve")
                        for i in range(4):
                            c = 4 * cg + i
                            _li = nc.tensor.transpose(
                                ps_t[:, 128 * i:128 * i + 128],
                                xn_sub[:, 128 * c:128 * c + 128], identb[:])
                        _li.then_inc(sems["pe"], 1)
                        pv = sems.inc("pe")
                        nc.vector.wait_ge(sems["pe"], pv)
                        for i in range(4):
                            c = 4 * cg + i
                            _lc = nc.vector.tensor_copy(
                                out=xnt_v[:, c, 128 * ms:128 * ms + 128],
                                in_=ps_t[:, 128 * i:128 * i + 128])
                        _lc.then_inc(sems["dve"], 1)
                        sems.inc("dve")
                    if also_xt:
                        for cg in range(8):
                            guard(nc.tensor, "dve")
                            for i in range(4):
                                c = 4 * cg + i
                                _li = nc.tensor.transpose(
                                    ps_tf[:, 128 * i:128 * i + 128],
                                    x_sub[:, 128 * c:128 * c + 128], identf[:])
                            _li.then_inc(sems["pe"], 1)
                            pv = sems.inc("pe")
                            nc.vector.wait_ge(sems["pe"], pv)
                            guard(nc.vector, "sv")
                            nc.vector.tensor_copy(
                                out=xt_stage[:], in_=ps_tf[:]).then_inc(sems["dve"], 1)
                            dv = sems.inc("dve")
                            nc.sync.wait_ge(sems["dve"], dv)
                            dst = xt_d.rearrange("(cc p) m -> p cc m", p=128)[
                                :, 4 * cg:4 * cg + 4, 128 * ms:128 * ms + 128]
                            dma(nc.sync, dst,
                                xt_stage[:].rearrange("p (cc m) -> p cc m", cc=4),
                                "sv")

            def load_wt_panel(bufv, par, wt_src, t):
                return dma(nc.sync, bufv[:, par], wt_src[t], "ld")

            def rope_apply(ps_src, half, out_bf):
                cs = cosb[:, 512 * half:512 * half + 512]
                sn = sinb[:, 512 * half:512 * half + 512]
                nc.vector.tensor_copy(out=kf[:], in_=ps_src[:]).then_inc(
                    sems["dve"], 1)
                dv = sems.inc("dve")
                nc.tensor.wait_ge(sems["dve"], dv)
                nc.tensor.matmul(ps_r[:], rotf[:], kf[:], start=True,
                                 stop=True).then_inc(sems["pe"], 1)
                pv = sems.inc("pe")
                nc.vector.tensor_mul(out=t1f[:], in0=kf[:], in1=cs)
                nc.vector.wait_ge(sems["pe"], pv)
                nc.vector.tensor_mul(out=t2f[:], in0=ps_r[:], in1=sn)
                nc.vector.tensor_add(out=out_bf, in0=t1f[:], in1=t2f[:]).then_inc(
                    sems["dve"], 1)
                return sems.inc("dve")

            if "A" in phases:
                for b in range(1, 8):
                    ln_block(b, also_xt=False)
                    guard(nc.sync, "dve")
                    dma(nc.sync, cosb[:], cos_in[:, BLK * b:BLK * b + BLK], "ld")
                    lv_cs = dma(nc.sync, sinb[:],
                                sin_in[:, BLK * b:BLK * b + BLK], "ld")
                    ln_dve = sems.val("dve")
                    kv_pe_mark = {}
                    kst_mark = {}
                    vst_mark = {}
                    for t in range(KVH):
                        par = t % 2
                        if t >= 2:
                            nc.sync.wait_ge(sems["pe"], kv_pe_mark[t - 2])
                        load_wt_panel(wpan_v, par, wkt_d, t)
                        lv = load_wt_panel(wpan2_v, par, wvt_d, t)
                        nc.tensor.wait_ge(sems["ld"], max(lv, lv_cs))
                        if t == 0:
                            nc.tensor.wait_ge(sems["dve"], ln_dve)
                        for half in (0, 1):
                            for c in range(ECH):
                                _li = nc.tensor.matmul(
                                    ps_k[:], wpan_v[:, par, c, :],
                                    xnt_v[:, c, 512 * half:512 * half + 512],
                                    start=(c == 0), stop=(c == ECH - 1))
                            _li.then_inc(sems["pe"], 1)
                            pv = sems.inc("pe")
                            nc.vector.wait_ge(sems["pe"], pv)
                            if kst_mark:
                                nc.vector.wait_ge(sems["sv"],
                                                  max(kst_mark.values()))
                            dvk = rope_apply(ps_k, half, kvb[:])
                            nc.sync.wait_ge(sems["dve"], dvk)
                            kst_mark[(t, half)] = dma(
                                nc.sync,
                                kt_d[128 * t:128 * t + 128,
                                     BLK * b + 512 * half:
                                     BLK * b + 512 * half + 512],
                                kvb[:], "sv")
                            for c in range(ECH):
                                _li = nc.tensor.matmul(
                                    ps_v[:], wpan2_v[:, par, c, :],
                                    xnt_v[:, c, 512 * half:512 * half + 512],
                                    start=(c == 0), stop=(c == ECH - 1))
                            _li.then_inc(sems["pe"], 1)
                            pv = sems.inc("pe")
                            nc.scalar.wait_ge(sems["pe"], pv)
                            nc.scalar.copy(out=vb_c[:], in_=ps_v[:]).then_inc(
                                sems["act"], 1)
                            av = sems.inc("act")
                            nc.tensor.wait_ge(sems["act"], av)
                            guard(nc.tensor, "dve")
                            for i in range(4):
                                _li = nc.tensor.transpose(
                                    ps_t[:, 128 * i:128 * i + 128],
                                    vb_c[:, 128 * i:128 * i + 128], identb[:])
                            _li.then_inc(sems["pe"], 1)
                            pv = sems.inc("pe")
                            if half == 1:
                                kv_pe_mark[t] = pv
                            nc.vector.wait_ge(sems["pe"], pv)
                            if vst_mark:
                                nc.vector.wait_ge(sems["sv"],
                                                  max(vst_mark.values()))
                            nc.vector.tensor_copy(
                                out=vstage[:], in_=ps_t[:]).then_inc(sems["dve"], 1)
                            dv = sems.inc("dve")
                            nc.sync.wait_ge(sems["dve"], dv)
                            bb0 = (BLK * b + 512 * half) // 128
                            dst = v_d.rearrange("kv bb p n -> kv p bb n")[
                                t, :, bb0:bb0 + 4, :]
                            vst_mark[(t, half)] = dma(
                                nc.sync, dst,
                                vstage[:].rearrange("p (ss n) -> p ss n", ss=4),
                                "sv")
                nc.sync.wait_ge(sems["sv"], sems.val("sv"))
                nc.all_engine_barrier()

            if "B1" in phases:
                ln_block(0, also_xt=True)
                guard(nc.sync, "dve")
                dma(nc.sync, cosb[:], cos_in[:, 0:BLK], "ld")
                lv_cs = dma(nc.sync, sinb[:], sin_in[:, 0:BLK], "ld")
                nc.sync.wait_ge(sems["sv"], sems.val("sv"))
                nc.all_engine_barrier()

            if "B2" in phases:
                q_pe_mark = {}
                ln_dve = sems.val("dve")
                for t in range(H):
                    par = t % 2
                    if t >= 2:
                        nc.sync.wait_ge(sems["pe"], q_pe_mark[t - 2])
                    lv = load_wt_panel(wpan_v, par, wqt_d, t)
                    nc.tensor.wait_ge(sems["ld"], max(lv, lv_cs))
                    if t == 0:
                        nc.tensor.wait_ge(sems["dve"], ln_dve)
                    for half in (0, 1):
                        for c in range(ECH):
                            _li = nc.tensor.matmul(
                                ps_k[:], wpan_v[:, par, c, :],
                                xnt_v[:, c, 512 * half:512 * half + 512],
                                start=(c == 0), stop=(c == ECH - 1))
                        _li.then_inc(sems["pe"], 1)
                        pv = sems.inc("pe")
                        if half == 1:
                            q_pe_mark[t] = pv
                        nc.vector.wait_ge(sems["pe"], pv)
                        rope_apply(ps_k, half,
                                   qt_v[:, t, 512 * half:512 * half + 512])
                if dbg:
                    guard(nc.sync, "dve")
                    for t in range(H):
                        dma(nc.sync, qt_dbg[128 * t:128 * t + 128, :],
                            qt_v[:, t, :], "sv")
                    nc.sync.wait_ge(sems["sv"], sems.val("sv"))
                nc.all_engine_barrier()

        if "B3" in phases:
            with ExitStack() as es:
                mask_sb = es.enter_context(nc.sbuf_tensor("mask_sb", [128, max(NMASK, 1) * 512], BF16))
                kt_all = es.enter_context(nc.sbuf_tensor("kt_all", [128, NVB * 128], BF16))
                v_all = es.enter_context(nc.sbuf_tensor("v_all", [128, NVB * 128], BF16))
                pt00 = es.enter_context(nc.sbuf_tensor("pt00", [128, 512], BF16))
                pt01 = es.enter_context(nc.sbuf_tensor("pt01", [128, 512], BF16))
                pt10 = es.enter_context(nc.sbuf_tensor("pt10", [128, 512], BF16))
                pt11 = es.enter_context(nc.sbuf_tensor("pt11", [128, 512], BF16))
                lrow = es.enter_context(nc.sbuf_tensor("lrow", [1, 1024], F32))
                rrow = es.enter_context(nc.sbuf_tensor("rrow", [1, 1024], F32))
                rb0 = es.enter_context(nc.sbuf_tensor("rb0", [128, 512], F32))
                rb1 = es.enter_context(nc.sbuf_tensor("rb1", [128, 512], F32))
                ps_s00 = es.enter_context(nc.psum_tensor("ps_s00", [128, 512], F32))
                ps_s01 = es.enter_context(nc.psum_tensor("ps_s01", [128, 512], F32))
                ps_s10 = es.enter_context(nc.psum_tensor("ps_s10", [128, 512], F32))
                ps_s11 = es.enter_context(nc.psum_tensor("ps_s11", [128, 512], F32))
                ps_o0 = es.enter_context(nc.psum_tensor("ps_o0", [128, 512], F32))
                ps_o1 = es.enter_context(nc.psum_tensor("ps_o1", [128, 512], F32))
                ps_l0 = es.enter_context(nc.psum_tensor("ps_l0", [128, 512], F32))
                ps_l1 = es.enter_context(nc.psum_tensor("ps_l1", [128, 512], F32))
                aot_v = big1[:].rearrange("p (c m) -> p c m", c=ECH)
                qt_v = big2[:].rearrange("p (c m) -> p c m", c=ECH)
                kt_v = kt_all[:].rearrange("p (b n) -> p b n", b=NVB)
                v_v = v_all[:].rearrange("p (b n) -> p b n", b=NVB)
                ps_s = [[ps_s00, ps_s01], [ps_s10, ps_s11]]
                pt = [[pt00, pt01], [pt10, pt11]]
                ps_o = [ps_o0, ps_o1]
                ps_l = [ps_l0, ps_l1]
                if NMASK:
                    dma(nc.gpsimd, mask_sb[:], masks_in, "ldg")
                    nc.gpsimd.wait_ge(sems["ldg"], sems.val("ldg"))
                nc.all_engine_barrier()

                for kv in range(KVH):
                    guard(nc.sync, "pe")
                    dma(nc.sync, kt_all[:], kt_d[128 * kv:128 * kv + 128, BLK:S],
                        "ld")
                    for bb in range(NVB):
                        lv = dma(nc.sync, v_v[:, bb, :], v_d[kv, 8 + bb], "ld")
                    nc.tensor.wait_ge(sems["ld"], lv)
                    for hq in range(4):
                        h = 4 * kv + hq
                        act_mark = {}
                        dve_mark = {}
                        masked_i = {}
                        pe_av = {}
                        guard(nc.scalar, "pe")
                        def emit_scores(i, Bb):
                            par = i % 2
                            if i >= 2:
                                nc.tensor.wait_ge(sems["act"], act_mark[i - 2])
                            for half in (0, 1):
                                nc.tensor.matmul(
                                    ps_s[half][par][:], kt_v[:, i, :],
                                    qt_v[:, h, 512 * half:512 * half + 512],
                                    start=True, stop=True).then_inc(sems["pe"], 1)
                                sems.inc("pe")
                            pe_s = sems.val("pe")
                            nc.scalar.wait_ge(sems["pe"], pe_s)
                            if i >= 2:
                                nc.scalar.wait_ge(sems["pe"], pe_av[i - 2])
                            for half in (0, 1):
                                nc.scalar.activation(
                                    out=pt[half][par][:], in_=ps_s[half][par][:],
                                    func=AF.Exp, bias=ebias[:, 0:1],
                                    scale=SCALE).then_inc(sems["act"], 1)
                                sems.inc("act")
                            act_mark[i] = sems.val("act")
                            masked = False
                            for half in (0, 1):
                                mi = MASK_IDX.get((Bb, half))
                                if mi is not None:
                                    nc.vector.wait_ge(sems["act"], act_mark[i])
                                    nc.vector.tensor_mul(
                                        out=pt[half][par][:],
                                        in0=pt[half][par][:],
                                        in1=mask_sb[:, 512 * mi:512 * mi + 512],
                                    ).then_inc(sems["dve"], 1)
                                    sems.inc("dve")
                                    masked = True
                            dve_mark[i] = sems.val("dve")
                            masked_i[i] = masked

                        def emit_lav(i):
                            par = i % 2
                            first, last = (i == 0), (i == 55)
                            if masked_i[i]:
                                nc.tensor.wait_ge(sems["dve"], dve_mark[i])
                            else:
                                nc.tensor.wait_ge(sems["act"], act_mark[i])
                            for half in (0, 1):
                                nc.tensor.matmul(
                                    ps_l[half][0:1, :], ones_col_b[:],
                                    pt[half][par][:], start=first, stop=last)
                            for half in (0, 1):
                                _li = nc.tensor.matmul(
                                    ps_o[half][:], v_v[:, i, :], pt[half][par][:],
                                    start=first, stop=last)
                            _li.then_inc(sems["pe"], 1)
                            pe_av[i] = sems.inc("pe")

                        for i, Bb in enumerate(range(8, 64)):
                            emit_scores(i, Bb)
                            if i >= 1:
                                emit_lav(i - 1)
                        emit_lav(55)
                        nc.vector.wait_ge(sems["pe"], pe_av[55])
                        nc.vector.tensor_copy(out=lrow[:, 0:512],
                                              in_=ps_l0[0:1, :])
                        nc.vector.tensor_copy(out=lrow[:, 512:1024],
                                              in_=ps_l1[0:1, :])
                        nc.vector.reciprocal(out=rrow[:], in_=lrow[:]).then_inc(
                            sems["dve"], 1)
                        dv = sems.inc("dve")
                        nc.tensor.wait_ge(sems["dve"], dv)
                        nc.tensor.matmul(ps_l0[:], ones_row_f[:], rrow[:, 0:512],
                                         start=True, stop=True)
                        nc.tensor.matmul(ps_l1[:], ones_row_f[:],
                                         rrow[:, 512:1024], start=True,
                                         stop=True).then_inc(sems["pe"], 1)
                        pv = sems.inc("pe")
                        nc.vector.wait_ge(sems["pe"], pv)
                        nc.vector.tensor_copy(out=rb0[:], in_=ps_l0[:])
                        nc.vector.tensor_copy(out=rb1[:], in_=ps_l1[:])
                        nc.vector.tensor_mul(out=aot_v[:, h, 0:512],
                                             in0=ps_o0[:], in1=rb0[:])
                        nc.vector.tensor_mul(out=aot_v[:, h, 512:1024],
                                             in0=ps_o1[:], in1=rb1[:]).then_inc(
                            sems["dve"], 1)
                        dve_oev = sems.inc("dve")
                        nc.tensor.wait_ge(sems["dve"], dve_oev)
                if dbg:
                    guard(nc.sync, "dve")
                    for t in range(H):
                        dma(nc.sync, aot_dbg[128 * t:128 * t + 128, :],
                            aot_v[:, t, :], "sv")
                    nc.sync.wait_ge(sems["sv"], sems.val("sv"))
            nc.all_engine_barrier()

        if "B4" in phases:
            with ExitStack() as es:
                wpanB = es.enter_context(nc.sbuf_tensor("wpanB", [128, 2 * ECH * 128], BF16))
                xt_tile = es.enter_context(nc.sbuf_tensor("xt_tile", [128, 2 * 512], F32))
                xm_tile = es.enter_context(nc.sbuf_tensor("xm_tile", [128, 2 * 512], F32))
                sqst = es.enter_context(nc.sbuf_tensor("sqst", [128, 512], F32))
                mrow = es.enter_context(nc.sbuf_tensor("mrow", [1, 512], F32))
                qrow = es.enter_context(nc.sbuf_tensor("qrow", [1, 512], F32))
                vrow = es.enter_context(nc.sbuf_tensor("vrow", [1, 512], F32))
                nmrow = es.enter_context(nc.sbuf_tensor("nmrow", [1, 512], F32))
                rsrow = es.enter_context(nc.sbuf_tensor("rsrow", [1, 512], F32))
                nb_b = es.enter_context(nc.sbuf_tensor("nb_b", [128, 512], F32))
                rs_b = es.enter_context(nc.sbuf_tensor("rs_b", [128, 512], F32))
                xn2st = es.enter_context(nc.sbuf_tensor("xn2st", [128, 512], BF16))
                aot_v = big1[:].rearrange("p (c m) -> p c m", c=ECH)
                wpanB_v = wpanB[:].rearrange("p (b c n) -> p b c n", b=2, c=ECH)
                for half in (0, 1):
                    with (
                        nc.psum_tensor(f"ps_oo{half}", [128, 512], F32) as ps_oo,
                        nc.psum_tensor(f"ps_sum{half}", [1, 512], F32) as ps_sum,
                        nc.psum_tensor(f"ps_sq{half}", [1, 512], F32) as ps_sq,
                    ):
                        oo_pe_mark = {}
                        xm_sv_mark = {}
                        for et in range(ECH):
                            par = et % 2
                            po = 512 * par
                            if et >= 2:
                                nc.sync.wait_ge(sems["pe"], oo_pe_mark[et - 2])
                            dma(nc.sync, wpanB_v[:, par], wot_d[et], "ld")
                            lv = dma(nc.sync, xt_tile[:, po:po + 512],
                                     xt_d[128 * et:128 * et + 128,
                                          512 * half:512 * half + 512], "ld")
                            nc.tensor.wait_ge(sems["ld"], lv)
                            for c in range(ECH):
                                _li = nc.tensor.matmul(
                                    ps_oo[:], wpanB_v[:, par, c, :],
                                    aot_v[:, c, 512 * half:512 * half + 512],
                                    start=(c == 0), stop=(c == ECH - 1))
                            _li.then_inc(sems["pe"], 1)
                            pv = sems.inc("pe")
                            nc.vector.wait_ge(sems["pe"], pv)
                            if et >= 2 and (et - 2) in xm_sv_mark:
                                nc.vector.wait_ge(sems["sv"], xm_sv_mark[et - 2])
                            nc.vector.tensor_add(
                                out=xm_tile[:, po:po + 512], in0=ps_oo[:],
                                in1=xt_tile[:, po:po + 512]).then_inc(
                                sems["dve"], 1)
                            dv = sems.inc("dve")
                            nc.sync.wait_ge(sems["dve"], dv)
                            xm_sv_mark[et] = dma(
                                nc.sync,
                                xmid_d[128 * et:128 * et + 128,
                                       512 * half:512 * half + 512],
                                xm_tile[:, po:po + 512], "sv")
                            nc.scalar.wait_ge(sems["dve"], dv)
                            nc.scalar.square(out=sqst[:],
                                             in_=xm_tile[:, po:po + 512]).then_inc(
                                sems["act"], 1)
                            av = sems.inc("act")
                            nc.tensor.wait_ge(sems["act"], av)
                            nc.tensor.matmul(ps_sum[:], ones_col_f[:],
                                             xm_tile[:, po:po + 512],
                                             start=(et == 0),
                                             stop=(et == ECH - 1))
                            nc.tensor.matmul(ps_sq[:], ones_col_f[:], sqst[:],
                                             start=(et == 0),
                                             stop=(et == ECH - 1)).then_inc(
                                sems["pe"], 1)
                            oo_pe_mark[et] = sems.inc("pe")
                        guard(nc.vector, "pe")
                        nc.vector.tensor_scalar(out=mrow[:], in0=ps_sum[0:1, :],
                                                scalar1=1.0 / E, scalar2=None,
                                                op0=ALU.mult)
                        nc.vector.tensor_scalar(out=qrow[:], in0=ps_sq[0:1, :],
                                                scalar1=1.0 / E, scalar2=None,
                                                op0=ALU.mult)
                        nc.vector.tensor_mul(out=vrow[:], in0=mrow[:], in1=mrow[:])
                        nc.vector.tensor_sub(out=vrow[:], in0=qrow[:], in1=vrow[:])
                        nc.vector.tensor_scalar(out=vrow[:], in0=vrow[:],
                                                scalar1=EPS, scalar2=None,
                                                op0=ALU.add)
                        nc.vector.tensor_scalar(out=nmrow[:], in0=mrow[:],
                                                scalar1=-1.0, scalar2=None,
                                                op0=ALU.mult).then_inc(sems["dve"], 1)
                        dv = sems.inc("dve")
                        nc.scalar.wait_ge(sems["dve"], dv)
                        nc.scalar.activation(out=vrow[:], in_=vrow[:], func=AF.Ln)
                        nc.scalar.activation(out=rsrow[:], in_=vrow[:], func=AF.Exp,
                                             scale=-0.5).then_inc(sems["act"], 1)
                        av = sems.inc("act")
                        nc.tensor.wait_ge(sems["act"], av)
                    with (
                        nc.psum_tensor(f"ps_b0{half}", [128, 512], F32) as ps_b0,
                        nc.psum_tensor(f"ps_b1{half}", [128, 512], F32) as ps_b1,
                    ):
                        nc.tensor.matmul(ps_b0[:], ones_row_f[:], nmrow[:],
                                         start=True, stop=True)
                        nc.tensor.matmul(ps_b1[:], ones_row_f[:], rsrow[:],
                                         start=True, stop=True).then_inc(
                            sems["pe"], 1)
                        pv = sems.inc("pe")
                        nc.vector.wait_ge(sems["pe"], pv)
                        nc.vector.tensor_copy(out=nb_b[:], in_=ps_b0[:])
                        nc.vector.tensor_copy(out=rs_b[:], in_=ps_b1[:]).then_inc(
                            sems["dve"], 1)
                        sems.inc("dve")
                        for et in range(ECH):
                            guard(nc.sync, "dve")
                            lv = dma(nc.sync, xm_tile[:, 0:512],
                                     xmid_d[128 * et:128 * et + 128,
                                            512 * half:512 * half + 512], "ld")
                            nc.vector.wait_ge(sems["ld"], lv)
                            guard(nc.vector, "sv")
                            nc.vector.tensor_add(out=sqst[:],
                                                 in0=xm_tile[:, 0:512],
                                                 in1=nb_b[:])
                            nc.vector.tensor_mul(out=xn2st[:], in0=sqst[:],
                                                 in1=rs_b[:]).then_inc(
                                sems["dve"], 1)
                            dv = sems.inc("dve")
                            nc.sync.wait_ge(sems["dve"], dv)
                            dma(nc.sync,
                                xn2_d[128 * et:128 * et + 128,
                                      512 * half:512 * half + 512],
                                xn2st[:], "sv")
                nc.sync.wait_ge(sems["sv"], sems.val("sv"))
            nc.all_engine_barrier()

        if "C1" in phases:
            with ExitStack() as es:
                indgu = es.enter_context(nc.sbuf_tensor("indgu", [32, 4 * 128], BF16))
                qwg_sb = es.enter_context(nc.sbuf_tensor("qwg_sb", [128, 4 * 512], I32))
                qwu_sb = es.enter_context(nc.sbuf_tensor("qwu_sb", [128, 4 * 512], I32))
                scg_b = es.enter_context(nc.sbuf_tensor("scg_b", [32, 512], BF16))
                scu_b = es.enter_context(nc.sbuf_tensor("scu_b", [32, 512], BF16))
                qzg_sb = es.enter_context(nc.sbuf_tensor("qzg_sb", [32, 64], I32))
                qzu_sb = es.enter_context(nc.sbuf_tensor("qzu_sb", [32, 64], I32))
                z4g_i = es.enter_context(nc.sbuf_tensor("z4g_i", [32, 512], I32))
                z4u_i = es.enter_context(nc.sbuf_tensor("z4u_i", [32, 512], I32))
                z4g_b = es.enter_context(nc.sbuf_tensor("z4g_b", [32, 512], BF16))
                z4u_b = es.enter_context(nc.sbuf_tensor("z4u_b", [32, 512], BF16))
                sbg = es.enter_context(nc.sbuf_tensor("sbg", [128, 512], BF16))
                zbg = es.enter_context(nc.sbuf_tensor("zbg", [128, 512], BF16))
                sbu = es.enter_context(nc.sbuf_tensor("sbu", [128, 512], BF16))
                zbu = es.enter_context(nc.sbuf_tensor("zbu", [128, 512], BF16))
                ui32 = es.enter_context(nc.sbuf_tensor("ui32", [128, 2 * 512], I32))
                wmz = es.enter_context(nc.sbuf_tensor("wmz", [128, 2 * 512], BF16))
                sg8 = es.enter_context(nc.sbuf_tensor("sg8", [128, 8 * 512], BF16))
                hstage = es.enter_context(nc.sbuf_tensor("hstage", [128, 2 * 512], BF16))
                ps_bc = es.enter_context(nc.psum_tensor("ps_bc", [128, 512], F32))
                ps_g0 = es.enter_context(nc.psum_tensor("ps_g0", [128, 512], F32))
                ps_g1 = es.enter_context(nc.psum_tensor("ps_g1", [128, 512], F32))
                xn2t_v = big1[:].rearrange("p (c m) -> p c m", c=ECH)
                wg_v = big2[:, 0:16384].rearrange("p (c n) -> p c n", c=ECH)
                wu_v = big2[:, 16384:32768].rearrange("p (c n) -> p c n", c=ECH)
                xn2_pl = xn2_d.rearrange("(r j) m -> j r m", j=8)
                for j in range(8):
                    for R in range(4):
                        c = 4 * j + R
                        dma(nc.sync, xn2t_v[:, c, :],
                            xn2_pl[j, 128 * R:128 * R + 128, :], "ld")
                dma(nc.gpsimd, indgu[:], indtgu_in, "ldg")
                nc.sync.wait_ge(sems["ld"], sems.val("ld"))
                nc.gpsimd.wait_ge(sems["ldg"], sems.val("ldg"))
                nc.all_engine_barrier()

                qwg_v = qwg_sb[:].rearrange("p (r n) -> p r n", r=4)
                qwu_v = qwu_sb[:].rearrange("p (r n) -> p r n", r=4)

                ps_gp = [ps_g0, ps_g1]
                stages = []
                for p in range(NPAN):
                    stages.append((p, "g"))
                    stages.append((p, "u"))
                wbuf_free = {}
                pool_done = {}
                bank_free = [None, None]
                sg_slot = {}
                sg_free = {}
                hst_free = [0, 0]
                dq_n = 0
                pool_hist = {}
                mm_n = 0

                def emit_dequant(si):
                    nonlocal dq_n
                    p, w = stages[si]
                    c0 = 512 * p
                    W = min(512, INTER - c0)
                    qw_src = qw_g if w == "g" else qw_u
                    sc_src = sc_g if w == "g" else sc_u
                    qz_src = qz_g if w == "g" else qz_u
                    qv = qwg_v if w == "g" else qwu_v
                    scb = scg_b if w == "g" else scu_b
                    qzb = qzg_sb if w == "g" else qzu_sb
                    z4i_ = z4g_i if w == "g" else z4u_i
                    z4b = z4g_b if w == "g" else z4u_b
                    sb_ = sbg if w == "g" else sbu
                    zb = zbg if w == "g" else zbu
                    wv = wg_v if w == "g" else wu_v
                    if w in wbuf_free:
                        nc.sync.wait_ge(sems["pe"], wbuf_free[w])
                        nc.gpsimd.wait_ge(sems["pe"], wbuf_free[w])
                    for R in range(4):
                        dma(nc.sync, qv[:, R, 0:W],
                            qw_src[128 * R:128 * R + 128, c0:c0 + W], "ld")
                    dma(nc.gpsimd, scb[:, 0:W], sc_src[:, c0:c0 + W], "ldg")
                    lv = dma(nc.sync, qzb[:, 0:W // 8],
                             qz_src[:, c0 // 8:c0 // 8 + W // 8], "ld")
                    lg = sems.val("ldg")
                    nc.vector.wait_ge(sems["ld"], lv)
                    nc.vector.wait_ge(sems["ldg"], lg)
                    for j in range(8):
                        nc.vector.tensor_scalar(
                            out=z4i_[:, j:W:8], in0=qzb[:, 0:W // 8],
                            scalar1=4 * j, scalar2=15,
                            op0=ALU.logical_shift_right, op1=ALU.bitwise_and)
                    nc.vector.tensor_scalar(
                        out=z4b[:, 0:W], in0=z4i_[:, 0:W], scalar1=1.0,
                        scalar2=None, op0=ALU.add).then_inc(sems["dve"], 1)
                    dv_z = sems.inc("dve")
                    nc.tensor.wait_ge(sems["dve"], dv_z)
                    for R in range(4):
                        for (src_b, dst_b) in ((scb, sb_), (z4b, zb)):
                            nc.tensor.matmul(
                                ps_bc[:, 0:W], indgu[:, 128 * R:128 * R + 128],
                                src_b[:, 0:W], start=True,
                                stop=True).then_inc(sems["pe"], 1)
                            pv = sems.inc("pe")
                            nc.vector.wait_ge(sems["pe"], pv)
                            if dq_n >= 1:
                                nc.vector.wait_ge(sems["pool"],
                                                  pool_hist[dq_n - 1])
                            nc.vector.tensor_copy(
                                out=dst_b[:, 0:W], in_=ps_bc[:, 0:W]).then_inc(
                                sems["dve"], 1)
                            dv = sems.inc("dve")
                            nc.tensor.wait_ge(sems["dve"], dv)
                        for j in range(8):
                            c = 4 * j + R
                            par = dq_n % 2
                            poff = 512 * par
                            if dq_n >= 2:
                                nc.vector.wait_ge(sems["pool"],
                                                  pool_hist[dq_n - 2])
                            nc.vector.tensor_scalar(
                                out=ui32[:, poff:poff + W], in0=qv[:, R, 0:W],
                                scalar1=4 * j, scalar2=15,
                                op0=ALU.logical_shift_right,
                                op1=ALU.bitwise_and)
                            nc.vector.tensor_sub(
                                out=wmz[:, poff:poff + W],
                                in0=ui32[:, poff:poff + W],
                                in1=zb[:, 0:W]).then_inc(sems["dve"], 1)
                            dv = sems.inc("dve")
                            nc.gpsimd.wait_ge(sems["dve"], dv)
                            nc.gpsimd.tensor_mul(
                                out=wv[:, c, 0:W], in0=wmz[:, poff:poff + W],
                                in1=sb_[:, 0:W]).then_inc(sems["pool"], 1)
                            pool_hist[dq_n] = sems.inc("pool")
                            dq_n += 1
                    pool_done[si] = sems.val("pool")

                def emit_mms(si):
                    nonlocal mm_n
                    p, w = stages[si]
                    c0 = 512 * p
                    W = min(512, INTER - c0)
                    wv = wg_v if w == "g" else wu_v
                    nc.tensor.wait_ge(sems["pool"], pool_done[si])
                    for tt in range(W // 128):
                        it = 4 * p + tt
                        for half in (0, 1):
                            gpar = mm_n % 2
                            mm_n += 1
                            psd = ps_gp[gpar]
                            if bank_free[gpar] is not None:
                                bsem, bmark = bank_free[gpar]
                                nc.tensor.wait_ge(sems[bsem], bmark)
                            for c in range(ECH):
                                _li = nc.tensor.matmul(
                                    psd[:], wv[:, c, 128 * tt:128 * tt + 128],
                                    xn2t_v[:, c, 512 * half:512 * half + 512],
                                    start=(c == 0), stop=(c == ECH - 1))
                            _li.then_inc(sems["pe"], 1)
                            pv = sems.inc("pe")
                            if w == "g":
                                nc.scalar.wait_ge(sems["pe"], pv)
                                off = 512 * (2 * tt + half)
                                if off in sg_free:
                                    nc.scalar.wait_ge(sems["dve"], sg_free[off])
                                nc.scalar.activation(
                                    out=sg8[:, off:off + 512], in_=psd[:],
                                    func=AF.Silu).then_inc(sems["act"], 1)
                                am = sems.inc("act")
                                bank_free[gpar] = ("act", am)
                                sg_slot[(tt, half)] = (off, am)
                            else:
                                off, sg_am = sg_slot[(tt, half)]
                                hs = mm_n % 2
                                hoff = 512 * hs
                                nc.vector.wait_ge(sems["pe"], pv)
                                nc.vector.wait_ge(sems["act"], sg_am)
                                if hst_free[hs]:
                                    nc.vector.wait_ge(sems["sv"], hst_free[hs])
                                nc.vector.tensor_mul(
                                    out=hstage[:, hoff:hoff + 512],
                                    in0=sg8[:, off:off + 512],
                                    in1=psd[:]).then_inc(sems["dve"], 1)
                                dv = sems.inc("dve")
                                bank_free[gpar] = ("dve", dv)
                                sg_free[off] = dv
                                nc.sync.wait_ge(sems["dve"], dv)
                                hst_free[hs] = dma(
                                    nc.sync,
                                    ht_d[128 * it:128 * it + 128,
                                         512 * half:512 * half + 512],
                                    hstage[:, hoff:hoff + 512], "sv")
                    wbuf_free[w] = sems.val("pe")

                for si in range(len(stages)):
                    emit_dequant(si)
                    if si >= 1:
                        emit_mms(si - 1)
                emit_mms(len(stages) - 1)
                nc.sync.wait_ge(sems["sv"], sems.val("sv"))
            nc.all_engine_barrier()

        if "C2" in phases:
            with ExitStack() as es:
                indd = es.enter_context(nc.sbuf_tensor("indd", [86, RTD * 128], BF16))
                scd_b = es.enter_context(nc.sbuf_tensor("scd_b", [86, 512], BF16))
                qzd_sb = es.enter_context(nc.sbuf_tensor("qzd_sb", [86, 64], I32))
                z4d_i = es.enter_context(nc.sbuf_tensor("z4d_i", [86, 512], I32))
                z4d_b = es.enter_context(nc.sbuf_tensor("z4d_b", [86, 512], BF16))
                sbd = es.enter_context(nc.sbuf_tensor("sbd", [128, RTD * 512], BF16))
                zbd = es.enter_context(nc.sbuf_tensor("zbd", [128, RTD * 512], BF16))
                qwd_sb = es.enter_context(nc.sbuf_tensor("qwd_sb", [128, 2 * 512], I32))
                ud32 = es.enter_context(nc.sbuf_tensor("ud32", [128, 2 * 512], I32))
                wmzd = es.enter_context(nc.sbuf_tensor("wmzd", [128, 2 * 512], BF16))
                wdd = es.enter_context(nc.sbuf_tensor("wdd", [128, 2 * 512], BF16))
                htc = es.enter_context(nc.sbuf_tensor("htc", [128, 2 * 1024], BF16))
                xmst = es.enter_context(nc.sbuf_tensor("xmst", [128, 512], F32))
                fo8 = es.enter_context(nc.sbuf_tensor("fo8", [128, 4096], F32))
                natst = es.enter_context(nc.sbuf_tensor("natst", [128, 512], F32))
                dma(nc.gpsimd, indd[:], indtd_in, "ldg")
                nc.gpsimd.wait_ge(sems["ldg"], sems.val("ldg"))
                sbd_v = sbd[:].rearrange("p (r n) -> p r n", r=RTD)
                zbd_v = zbd[:].rearrange("p (r n) -> p r n", r=RTD)
                ht_pl = ht_d.rearrange("(r j) m -> j r m", j=8)
                for eg in range(8):
                    e0 = 512 * eg
                    guard(nc.sync, "pe", "dve")
                    dma(nc.gpsimd, scd_b[:], sc_d[:, e0:e0 + 512], "ldg")
                    lv = dma(nc.sync, qzd_sb[:], qz_d[:, e0 // 8:e0 // 8 + 64],
                             "ld")
                    nc.vector.wait_ge(sems["ld"], lv)
                    nc.vector.wait_ge(sems["ldg"], sems.val("ldg"))
                    for j in range(8):
                        nc.vector.tensor_scalar(
                            out=z4d_i[:, j::8], in0=qzd_sb[:], scalar1=4 * j,
                            scalar2=15, op0=ALU.logical_shift_right,
                            op1=ALU.bitwise_and)
                    nc.vector.tensor_scalar(out=z4d_b[:], in0=z4d_i[:],
                                            scalar1=1.0, scalar2=None,
                                            op0=ALU.add).then_inc(sems["dve"], 1)
                    dv = sems.inc("dve")
                    nc.tensor.wait_ge(sems["dve"], dv)
                    with nc.psum_tensor(f"ps_pre{eg}", [128, 512], F32) as ps_pre:
                        for rt in range(RTD):
                            for (src_b, dstv) in ((scd_b, sbd_v), (z4d_b, zbd_v)):
                                nc.tensor.matmul(
                                    ps_pre[:], indd[:, 128 * rt:128 * rt + 128],
                                    src_b[:], start=True,
                                    stop=True).then_inc(sems["pe"], 1)
                                pv = sems.inc("pe")
                                nc.vector.wait_ge(sems["pe"], pv)
                                nc.vector.tensor_copy(
                                    out=dstv[:, rt, :], in_=ps_pre[:]).then_inc(
                                    sems["dve"], 1)
                                dv = sems.inc("dve")
                                nc.tensor.wait_ge(sems["dve"], dv)
                    with nc.psum_tensor(f"ps_d8{eg}", [128, 4096], F32) as ps_d8:
                        qwd_mark = {}
                        qwd_free = {}
                        htc_mark = {}
                        dve_mark = {}
                        pool_mark = {}
                        mm_mark = {}
                        nch = 0
                        for rt in range(RTD):
                            cnt = 128 if rt < 10 else RTD_TAIL
                            rpar = rt % 2
                            if rt >= 2 and (rt - 2) in qwd_free:
                                nc.sync.wait_ge(sems["dve"], qwd_free[rt - 2])
                            qwd_mark[rt] = dma(
                                nc.sync, qwd_sb[0:cnt, 512 * rpar:512 * rpar + 512],
                                qw_d[128 * rt:128 * rt + cnt, e0:e0 + 512], "ld")
                            for j in range(8):
                                n = nch
                                nch += 1
                                par = n % 2
                                first = (rt == 0 and j == 0)
                                last = (rt == RTD - 1 and j == 7)
                                if n >= 2:
                                    nc.sync.wait_ge(sems["pe"], mm_mark[n - 2])
                                htc_mark[n] = dma(
                                    nc.sync, htc[0:cnt, 1024 * par:1024 * par + 1024],
                                    ht_pl[j, 128 * rt:128 * rt + cnt, :], "ld")
                                nc.vector.wait_ge(sems["ld"], qwd_mark[rt])
                                if n >= 2:
                                    nc.vector.wait_ge(sems["pool"],
                                                      pool_mark[n - 2])
                                nc.vector.tensor_scalar(
                                    out=ud32[0:cnt, 512 * par:512 * par + 512],
                                    in0=qwd_sb[0:cnt, 512 * rpar:512 * rpar + 512],
                                    scalar1=4 * j, scalar2=15,
                                    op0=ALU.logical_shift_right,
                                    op1=ALU.bitwise_and)
                                nc.vector.tensor_sub(
                                    out=wmzd[0:cnt, 512 * par:512 * par + 512],
                                    in0=ud32[0:cnt, 512 * par:512 * par + 512],
                                    in1=zbd_v[0:cnt, rt, :]).then_inc(
                                    sems["dve"], 1)
                                dve_mark[n] = sems.inc("dve")
                                if j == 7:
                                    qwd_free[rt] = dve_mark[n]
                                nc.gpsimd.wait_ge(sems["dve"], dve_mark[n])
                                if n >= 2:
                                    nc.gpsimd.wait_ge(sems["pe"], mm_mark[n - 2])
                                nc.gpsimd.tensor_mul(
                                    out=wdd[0:cnt, 512 * par:512 * par + 512],
                                    in0=wmzd[0:cnt, 512 * par:512 * par + 512],
                                    in1=sbd_v[0:cnt, rt, :]).then_inc(
                                    sems["pool"], 1)
                                pool_mark[n] = sems.inc("pool")
                                nc.tensor.wait_ge(sems["pool"], pool_mark[n])
                                nc.tensor.wait_ge(sems["ld"], htc_mark[n])
                                for el in range(4):
                                    for half in (0, 1):
                                        off = 512 * (2 * el + half)
                                        _li = nc.tensor.matmul(
                                            ps_d8[:, off:off + 512],
                                            wdd[0:cnt, 512 * par + 128 * el:
                                                512 * par + 128 * el + 128],
                                            htc[0:cnt, 1024 * par + 512 * half:
                                                1024 * par + 512 * half + 512],
                                            start=first, stop=last)
                                _li.then_inc(sems["pe"], 1)
                                mm_mark[n] = sems.inc("pe")
                        for el in range(4):
                            et = 4 * eg + el
                            for half in (0, 1):
                                off = 512 * (2 * el + half)
                                guard(nc.sync, "dve")
                                lv = dma(nc.sync, xmst[:],
                                         xmid_d[128 * et:128 * et + 128,
                                                512 * half:512 * half + 512],
                                         "ld")
                                nc.vector.wait_ge(sems["ld"], lv)
                                guard(nc.vector, "pe")
                                nc.vector.tensor_add(
                                    out=fo8[:, off:off + 512],
                                    in0=ps_d8[:, off:off + 512],
                                    in1=xmst[:]).then_inc(sems["dve"], 1)
                                sems.inc("dve")
                    with nc.psum_tensor(f"ps_tr{eg}", [128, 512], F32) as ps_tr:
                        for el in range(4):
                            et = 4 * eg + el
                            for half in (0, 1):
                                off = 512 * (2 * el + half)
                                guard(nc.tensor, "dve")
                                for qq in range(4):
                                    _li = nc.tensor.transpose(
                                        ps_tr[:, 128 * qq:128 * qq + 128],
                                        fo8[:, off + 128 * qq:off + 128 * qq + 128],
                                        identf[:])
                                _li.then_inc(sems["pe"], 1)
                                pv = sems.inc("pe")
                                nc.vector.wait_ge(sems["pe"], pv)
                                guard(nc.vector, "sv")
                                nc.vector.tensor_copy(
                                    out=natst[:], in_=ps_tr[:]).then_inc(
                                    sems["dve"], 1)
                                dv = sems.inc("dve")
                                nc.sync.wait_ge(sems["dve"], dv)
                                for qq in range(4):
                                    dma(nc.sync,
                                        out[512 * half + 128 * qq:
                                            512 * half + 128 * qq + 128,
                                            128 * et:128 * et + 128],
                                        natst[:, 128 * qq:128 * qq + 128],
                                        "sv")
            nc.sync.wait_ge(sems["sv"], sems.val("sv"))
            nc.all_engine_barrier()

    return nc


def _in_maps(inputs):
    for nm, kdim in (("L0_ffn_gate_g_idx", E), ("L0_ffn_up_g_idx", E),
                     ("L0_ffn_down_g_idx", INTER)):
        gi = np.asarray(inputs[nm]).reshape(-1)
        if gi.shape[0] != kdim or not np.array_equal(
                gi, np.arange(kdim, dtype=np.int64) // 128):
            raise ValueError(f"{nm}: unsupported GPTQ group layout")
    cosT, sinT, ident, RT, indt_gu, indt_d = _consts()
    MASK_LIST, mask_tiles = _masks()
    x = np.ascontiguousarray(
        np.asarray(inputs["L0_attn_hidden_states"], dtype=np.float32).reshape(S, E))
    shared = {
        "wq": np.ascontiguousarray(np.asarray(inputs["L0_attn_Wq"], np.float32)),
        "wk": np.ascontiguousarray(np.asarray(inputs["L0_attn_Wk"], np.float32)),
        "wv": np.ascontiguousarray(np.asarray(inputs["L0_attn_Wv"], np.float32)),
        "wo": np.ascontiguousarray(np.asarray(inputs["L0_attn_Wo"], np.float32)),
        "qw_g": np.ascontiguousarray(np.asarray(inputs["L0_ffn_gate_qweight"], np.int32)),
        "sc_g": np.ascontiguousarray(np.asarray(inputs["L0_ffn_gate_scales"], np.float32)),
        "qz_g": np.ascontiguousarray(np.asarray(inputs["L0_ffn_gate_qzeros"], np.int32)),
        "qw_u": np.ascontiguousarray(np.asarray(inputs["L0_ffn_up_qweight"], np.int32)),
        "sc_u": np.ascontiguousarray(np.asarray(inputs["L0_ffn_up_scales"], np.float32)),
        "qz_u": np.ascontiguousarray(np.asarray(inputs["L0_ffn_up_qzeros"], np.int32)),
        "qw_d": np.ascontiguousarray(np.asarray(inputs["L0_ffn_down_qweight"], np.int32)),
        "sc_d": np.ascontiguousarray(np.asarray(inputs["L0_ffn_down_scales"], np.float32)),
        "qz_d": np.ascontiguousarray(np.asarray(inputs["L0_ffn_down_qzeros"], np.int32)),
        "ident": ident,
        "rotm": RT,
        "indt_gu": indt_gu,
        "indt_d": indt_d,
    }
    maps = []
    for c in range(NCORES):
        m = dict(shared)
        m["x_rot"] = np.ascontiguousarray(np.roll(x, -BLK * c, axis=0))
        m["cosT"] = np.ascontiguousarray(np.roll(cosT, -BLK * c, axis=1))
        m["sinT"] = np.ascontiguousarray(np.roll(sinT, -BLK * c, axis=1))
        nm = max(len(MASK_LIST), 1)
        mk = np.zeros((128, nm * 512), np.float32)
        for i in range(len(MASK_LIST)):
            mk[:, 512 * i:512 * i + 512] = mask_tiles[c][i]
        m["masks"] = mk
        maps.append(m)
    return maps


def kernel(**inputs):
    nc = build_program()
    maps = _in_maps(inputs)
    res = run_bass_kernel_spmd(nc, maps, core_ids=list(range(NCORES)))
    out = np.concatenate([res.results[c]["out"] for c in range(NCORES)], axis=0)
    return out.reshape(1, S, E).astype(np.float32)
